# version 01
# baseline (speedup 1.0000x reference)
import numpy as np
import concourse.bass as bass
import concourse.mybir as mybir
from concourse.bass_utils import run_bass_kernel_spmd

F32 = mybir.dt.float32
BF16 = mybir.dt.bfloat16
AF = mybir.ActivationFunctionType
ALU = mybir.AluOpType
AX = mybir.AxisListType

N_CORES = 8
B, T, D = 4, 4096, 1024
NT = B * T
TOK_PER_CORE = NT // N_CORES



class Sem:
    __slots__ = ("h", "n")

    def __init__(self, h):
        self.h = h
        self.n = 0


class Prog:

    ENGINES = ("tensor", "vector", "scalar", "gpsimd", "sync")

    def __init__(self, nc):
        self.nc = nc
        self.q = {e: [] for e in self.ENGINES}
        self._semid = 0

    def sem(self, name=None):
        h = self.nc.alloc_semaphore(name or f"s{self._semid}")
        self._semid += 1
        return Sem(h)

    def emit(self, eng, fn, wait=(), inc=()):
        wait = tuple((s, v) for s, v in wait)
        inc = tuple(inc)
        for s, a in inc:
            s.n += a

        def run(e, fn=fn, wait=wait, inc=inc):
            for s, v in wait:
                e.wait_ge(s.h, v)
            inst = fn(e)
            for s, a in inc:
                inst = inst.then_inc(s.h, a)

        self.q[eng].append(run)

    def wait(self, eng, sem, val):
        self.q[eng].append(lambda e, s=sem, v=val: e.wait_ge(s.h, v))

    def dma(self, eng, out, in_, sem, wait=()):
        return self.emit(eng, lambda e: e.dma_start(out=out, in_=in_),
                         wait=wait, inc=((sem, 16),))

    def finalize(self):
        with self.nc.Block() as block:
            for name in self.ENGINES:
                ops = self.q[name]
                if not ops:
                    continue
                getattr(block, name)(lambda eng, ops=ops: [op(eng) for op in ops])


def new_nc():
    return bass.Bass("TRN2", target_bir_lowering=False)


def setup_consts(p, values, gq):
    nc = p.nc
    for v in values:
        t = nc.alloc_sbuf_tensor(
            f"cst_{str(v).replace('.', '_').replace('-', 'm')}", [128, 1], F32)
        nc.const_aps.register(F32, float(v), t.ap())
        p.emit("gpsimd", lambda g, t=t, v=v: g.memset(t[:], v), inc=((gq, 1),))
    return gq.n



def emit_ln_tiles(p, n_tiles, load_tile, out_tile, ln, sems):
    Dd = ln["D"]
    ld, vch, sch = sems["ld"], sems["vch"], sems["sch"]
    done_tok = []
    for i in range(n_tiles):
        pb = i % 2
        x, xsq, st, xn = ln["x"][pb], ln["xsq"][pb], ln["st"][pb], ln["xn"][pb]
        waits = []
        if i >= 2:
            waits.append(done_tok[i - 2])
        load_tile(i, x, waits)
        ldv = ld.n
        p.emit("vector", lambda v, x=x, st=st:
               v.reduce_sum(st[:, 0:1], x[:], axis=AX.X),
               wait=[(ld, ldv)], inc=((vch, 1),))
        p.emit("scalar", lambda s, x=x, xsq=xsq, st=st:
               s.activation(xsq[:], x[:], AF.Square, accum_out=st[:, 1:2]),
               wait=[(ld, ldv)], inc=((sch, 1),))
        p.emit("vector", lambda v, st=st: v.tensor_scalar(
            st[:, 2:3], st[:, 0:1], 1.0 / Dd, None, ALU.mult),
            wait=[(vch, vch.n)], inc=((vch, 1),))
        p.emit("vector", lambda v, st=st: v.tensor_scalar(
            st[:, 3:4], st[:, 1:2], 1.0 / Dd, None, ALU.mult),
            wait=[(sch, sch.n)], inc=((vch, 1),))
        p.emit("vector", lambda v, st=st: v.tensor_tensor(
            st[:, 4:5], st[:, 2:3], st[:, 2:3], ALU.mult),
            wait=[(vch, vch.n)], inc=((vch, 1),))
        p.emit("vector", lambda v, st=st: v.tensor_tensor(
            st[:, 5:6], st[:, 3:4], st[:, 4:5], ALU.subtract),
            wait=[(vch, vch.n)], inc=((vch, 1),))
        p.emit("scalar", lambda s, st=st: s.activation(
            st[:, 6:7], st[:, 5:6], AF.Sqrt, bias=1e-5),
            wait=[(vch, vch.n)], inc=((sch, 1),))
        p.emit("vector", lambda v, st=st: v.reciprocal(st[:, 7:8], st[:, 6:7]),
               wait=[(sch, sch.n)], inc=((vch, 1),))
        p.emit("vector", lambda v, x=x, xn=xn, st=st: v.tensor_scalar(
            xn[:], x[:], st[:, 2:3], st[:, 7:8], ALU.subtract, ALU.mult),
            wait=[(vch, vch.n)], inc=((vch, 1),))
        done_tok.append(out_tile(i, xn, (vch, vch.n)))
    return done_tok



def build_l0_attn():
    nc = new_nc()
    p = Prog(nc)
    NTILE = T // 128

    xb = nc.dram_tensor("xb", [T, D], F32, kind="ExternalInput").ap()
    wqk = nc.dram_tensor("wqk", [8, D, 128], F32, kind="ExternalInput").ap()
    wv = nc.dram_tensor("wv", [8, D, 64], F32, kind="ExternalInput").ap()
    ident = nc.dram_tensor("ident", [128, 128], BF16, kind="ExternalInput").ap()
    triu = nc.dram_tensor("triu", [128, 128], BF16, kind="ExternalInput").ap()
    oT_out = nc.dram_tensor("oT", [4, 128, T], F32, kind="ExternalOutput").ap()
    xnT_out = nc.dram_tensor("xnT_o", [8, 128, T], BF16,
                             kind="ExternalOutput").ap()

    sb = nc.alloc_sbuf_tensor
    xnT = sb("xnT", [128, 8, T], BF16)
    qT = sb("qT", [64, T], BF16)
    kT = sb("kT", [64, T], BF16)
    vT = sb("vT", [64, T], BF16)
    vt = sb("vt", [128, NTILE, 65], BF16)
    ptb = [sb(f"ptb{i}", [128, 1024], BF16) for i in range(2)]
    osb = sb("osb", [128, 4, T], F32)
    lrow = sb("lrow", [1, 1024], F32)
    lrow2 = sb("lrow2", [1, 1024], F32)
    lrowb = sb("lrowb", [1, 1024], BF16)
    otmp = sb("otmp", [64, 1024], F32)
    ident_s = sb("ident_s", [128, 128], BF16)
    triu_s = sb("triu_s", [128, 128], BF16)
    ones1x64 = sb("ones1x64", [1, 64], BF16)
    wqk_s = [sb(f"wqk_s{i}", [128, 8, 128], BF16) for i in range(2)]
    wv_s = [sb(f"wv_s{i}", [128, 8, 64], BF16) for i in range(2)]
    lnb = {
        "D": D,
        "x": [sb(f"lnx{i}", [128, D], F32) for i in range(2)],
        "xsq": [sb(f"lnq{i}", [128, D], F32) for i in range(2)],
        "st": [sb(f"lns{i}", [128, 8], F32) for i in range(2)],
        "xn": [sb(f"lnn{i}", [128, D], BF16) for i in range(2)],
    }

    ps = nc.alloc_psum_tensor
    P1 = ps("P1", [128, 1024], F32)
    P2 = ps("P2", [128, 1024], F32)
    P3 = ps("P3", [128, 2048], BF16)
    P4 = ps("P4", [64, 1024], F32)

    ld = p.sem("ld")
    vch = p.sem("vch")
    sch = p.sem("sch")
    mm = p.sem("mm")
    tr = p.sem("tr")
    wld = p.sem("wld")
    gmc = p.sem("gmc")
    st = p.sem("st")

    cv = setup_consts(p, [-4.0, 0.125, 1e-5], gmc)
    p.wait("scalar", gmc, cv)
    p.dma("sync", ident_s[:], ident, ld)
    p.dma("sync", triu_s[:], triu, ld)
    p.emit("gpsimd", lambda g: g.memset(ones1x64[:], 1.0), inc=((gmc, 1),))
    p.emit("gpsimd", lambda g: g.memset(vt[:], 1.0), inc=((gmc, 1),))
    consts_ld = ld.n
    consts_g = gmc.n
    p.wait("tensor", gmc, consts_g)

    def load_tile(i, xbuf, waits):
        p.dma("sync", xbuf[:], xb[i * 128:(i + 1) * 128, :], ld, wait=waits)

    def out_tile(i, xn_ap, ready):
        for k in range(8):
            half = k % 2
            w = [ready, (ld, consts_ld)]
            if i > 0 or k >= 2:
                w.append((tr, tr.n - 1))
            p.emit("tensor", lambda t, xn_ap=xn_ap, k=k, half=half:
                   t.transpose(P3[:, half * 1024:half * 1024 + 128],
                               xn_ap[:, k * 128:(k + 1) * 128], ident_s[:]),
                   wait=w, inc=((mm, 1),))
            mv = mm.n
            p.emit("scalar", lambda s, i=i, k=k, half=half:
                   s.copy(xnT[:, k, i * 128:(i + 1) * 128],
                          P3[:, half * 1024:half * 1024 + 128]),
                   wait=[(mm, mv)], inc=((tr, 1),))
        return (tr, tr.n)

    emit_ln_tiles(p, NTILE, load_tile, out_tile, lnb,
                  dict(ld=ld, vch=vch, sch=sch))
    xnT_ready = (tr, tr.n)

    p.dma("gpsimd", wqk_s[0][:], wqk[0].rearrange("(k p) c -> p k c", p=128), wld)
    p.dma("gpsimd", wv_s[0][:], wv[0].rearrange("(k p) c -> p k c", p=128), wld)

    NQG = 4
    for h in range(8):
        hp = h % 2
        wq = wqk_s[hp]
        wvs = wv_s[hp]
        wldv = wld.n

        for ts in range(8):
            waits = [(wld, wldv), xnT_ready]
            if ts > 0:
                waits.append((tr, tr.n))
            for k in range(8):
                p.emit("tensor", lambda t, k=k, ts=ts, wq=wq:
                       t.matmul(P1[:, 0:512], wq[:, k, :],
                                xnT[:, k, ts * 512:(ts + 1) * 512],
                                start=(k == 0), stop=(k == 7)),
                       wait=waits if k == 0 else [],
                       inc=((mm, 1),) if k == 7 else ())
            mv = mm.n
            p.emit("scalar", lambda s, ts=ts:
                   s.copy(qT[:, ts * 512:(ts + 1) * 512], P1[0:64, 0:512]),
                   wait=[(mm, mv)], inc=((sch, 1),))
            p.emit("vector", lambda v, ts=ts:
                   v.tensor_copy(kT[:, ts * 512:(ts + 1) * 512], P1[64:128, 0:512]),
                   wait=[(mm, mv)], inc=((tr, 1),))
            p.wait("tensor", sch, sch.n)
        qk_done = (tr.n, sch.n)

        for ts in range(8):
            waits = [(tr, tr.n)]
            for k in range(8):
                p.emit("tensor", lambda t, k=k, ts=ts, wvs=wvs:
                       t.matmul(P1[0:64, 0:512], wvs[:, k, :],
                                xnT[:, k, ts * 512:(ts + 1) * 512],
                                start=(k == 0), stop=(k == 7)),
                       wait=waits if k == 0 else [],
                       inc=((mm, 1),) if k == 7 else ())
            mv = mm.n
            p.emit("vector", lambda v, ts=ts:
                   v.tensor_copy(vT[:, ts * 512:(ts + 1) * 512], P1[0:64, 0:512]),
                   wait=[(mm, mv)], inc=((tr, 1),))
        v_done_tr = tr.n

        if h < 7:
            p.dma("gpsimd", wqk_s[(h + 1) % 2][:],
                  wqk[h + 1].rearrange("(k p) c -> p k c", p=128), wld,
                  wait=[(mm, mm.n)])
            p.dma("gpsimd", wv_s[(h + 1) % 2][:],
                  wv[h + 1].rearrange("(k p) c -> p k c", p=128), wld,
                  wait=[(mm, mm.n)])

        for j in range(NTILE):
            half = j % 2
            w = [(tr, v_done_tr)] if j < 2 else [(tr, tr.n - 1)]
            p.emit("tensor", lambda t, j=j, half=half:
                   t.transpose(P3[:, half * 1024:half * 1024 + 64],
                               vT[:, j * 128:(j + 1) * 128],
                               ident_s[0:64, 0:64]),
                   wait=w, inc=((mm, 1),))
            p.emit("scalar", lambda s, j=j, half=half:
                   s.copy(vt[:, j, 0:64], P3[:, half * 1024:half * 1024 + 64]),
                   wait=[(mm, mm.n)], inc=((tr, 1),))
        vt_done = tr.n

        pO = P2[0:65, :]
        for g in range(NQG):
            ilo, ihi = 8 * g, 8 * g + 8
            for j in range(ihi):
                i_start = max(j, ilo)
                nq = (ihi - i_start) * 128
                tq0 = i_start * 128
                ptile = ptb[j % 2]
                nchunks = (nq + 511) // 512
                for c in range(nchunks):
                    cw = min(512, nq - c * 512)
                    waits = []
                    if c == 0:
                        if g == 0 and j == 0:
                            waits.append((tr, vt_done))
                        waits.append((sch, sch.n))
                    p.emit("tensor", lambda t, c=c, cw=cw, tq0=tq0, j=j:
                           t.matmul(P1[:, c * 512:c * 512 + cw],
                                    kT[:, j * 128:(j + 1) * 128],
                                    qT[:, tq0 + c * 512:tq0 + c * 512 + cw],
                                    start=True, stop=True),
                           wait=waits, inc=((mm, 1),))
                mmv = mm.n
                p.emit("scalar", lambda s, ptile=ptile, nq=nq:
                       s.activation(ptile[:, 0:nq], P1[:, 0:nq], AF.Exp,
                                    bias=-4.0, scale=0.125),
                       wait=[(mm, mmv)], inc=((sch, 1),))
                expv = sch.n
                pv_wait = [(sch, expv)]
                if j >= ilo:
                    p.emit("vector", lambda v, ptile=ptile:
                           v.tensor_tensor(ptile[:, 0:128], ptile[:, 0:128],
                                           triu_s[:], ALU.mult),
                           wait=[(sch, expv), (ld, consts_ld)], inc=((vch, 1),))
                    pv_wait = [(vch, vch.n)]
                if j == 0:
                    for c in range(2):
                        p.emit("tensor", lambda t, c=c, ptile=ptile:
                               t.matmul(pO[:, c * 512:(c + 1) * 512], vt[:, 0, :],
                                        ptile[:, c * 512:(c + 1) * 512],
                                        start=True, stop=False),
                               wait=pv_wait if c == 0 else [],
                               inc=((mm, 1),) if c == 1 else ())
                else:
                    n_i = nq // 128
                    for ii in range(n_i):
                        iblk = i_start + ii
                        col0 = (iblk - ilo) * 128
                        p.emit("tensor", lambda t, ii=ii, col0=col0, j=j,
                               iblk=iblk, ptile=ptile:
                               t.matmul(pO[:, col0:col0 + 128], vt[:, j, :],
                                        ptile[:, ii * 128:(ii + 1) * 128],
                                        start=False, stop=(j == iblk)),
                               wait=pv_wait if ii == 0 else [],
                               inc=((mm, 1),) if ii == n_i - 1 else ())
            grp_done = mm.n
            p.emit("vector", lambda v: v.tensor_copy(lrow[:], pO[64:65, :]),
                   wait=[(mm, grp_done)], inc=((vch, 1),))
            p.emit("vector", lambda v: v.reciprocal(lrow2[:], lrow[:]),
                   wait=[(vch, vch.n)], inc=((vch, 1),))
            p.emit("vector", lambda v: v.tensor_copy(lrowb[:], lrow2[:]),
                   wait=[(vch, vch.n)], inc=((vch, 1),))
            linv_v = vch.n
            p.emit("tensor", lambda t: t.matmul(
                P4[:, 0:512], ones1x64[:], lrowb[:, 0:512], start=True, stop=True),
                wait=[(vch, linv_v), (ld, consts_ld)])
            p.emit("tensor", lambda t: t.matmul(
                P4[:, 512:1024], ones1x64[:], lrowb[:, 512:1024],
                start=True, stop=True), inc=((mm, 1),))
            bc_v = mm.n
            p.emit("scalar", lambda s: s.copy(otmp[:], pO[0:64, :]),
                   wait=[(mm, bc_v)], inc=((sch, 1),))
            p.emit("vector", lambda v, h=h, g=g: v.tensor_tensor(
                osb[(h % 2) * 64:(h % 2) * 64 + 64, h // 2,
                    g * 1024:(g + 1) * 1024],
                otmp[:], P4[:, :], ALU.mult),
                wait=[(sch, sch.n)], inc=((vch, 1),))
            p.wait("tensor", sch, sch.n)
            p.wait("tensor", vch, vch.n)
    osb_done = vch.n

    p.wait("sync", vch, osb_done)
    p.wait("sync", sch, sch.n)
    for c in range(4):
        p.dma("sync", oT_out[c], osb[:, c, :], st)
    p.wait("sync", st, st.n)

    p.finalize()
    return nc



def build_l0_ffn():
    nc = new_nc()
    p = Prog(nc)
    TOK = TOK_PER_CORE
    NSLAB = TOK // 512
    F = 4096
    NF = F // 128

    xnT_in = nc.dram_tensor("xnT", [8, 128, TOK], BF16, kind="ExternalInput").ap()
    w1 = nc.dram_tensor("w1", [D, F], F32, kind="ExternalInput").ap()
    b1 = nc.dram_tensor("b1", [128, F // 128], F32, kind="ExternalInput").ap()
    w2 = nc.dram_tensor("w2", [F, D], F32, kind="ExternalInput").ap()
    b2 = nc.dram_tensor("b2", [1, D], F32, kind="ExternalInput").ap()
    eo_out = nc.dram_tensor("eo", [TOK, D], F32, kind="ExternalOutput").ap()

    sb = nc.alloc_sbuf_tensor
    w1s = sb("w1s", [128, 8, F], BF16)
    w2s = sb("w2s", [128, NF, D], BF16)
    b1s = sb("b1s", [128, NF], F32)
    b2s = sb("b2s", [1, D], BF16)
    ones1 = sb("ones1", [1, 128], BF16)
    xns = [sb(f"xns{i}", [128, 8, 512], BF16) for i in range(2)]
    h1T = sb("h1T", [128, NF, 512], BF16)
    eos = [sb(f"eos{i}", [128, D], F32) for i in range(2)]

    ps = nc.alloc_psum_tensor
    PH = [ps(f"PH{i}", [128, 512], F32) for i in range(2)]
    PE_ = [ps(f"PE{i}", [128, D], F32) for i in range(2)]

    ld = p.sem("ld")
    gld = p.sem("gld")
    gmc = p.sem("gmc")
    mm = p.sem("mm")
    ev = p.sem("ev")
    sto = p.sem("sto")

    p.dma("gpsimd", w1s[:], w1.rearrange("(k p) f -> p k f", p=128), gld)
    p.dma("gpsimd", w2s[:], w2.rearrange("(f p) c -> p f c", p=128), gld)
    p.dma("sync", b1s[:], b1, ld)
    p.dma("gpsimd", b2s[:], b2, gld)
    p.emit("gpsimd", lambda g: g.memset(ones1[:], 1.0), inc=((gmc, 1),))
    wld = ld.n
    p.wait("tensor", gld, gld.n)
    p.wait("tensor", gmc, gmc.n)
    p.wait("vector", gld, gld.n)
    eo_evac_vals = []

    for s in range(NSLAB):
        xs = xns[s % 2]
        w = [(mm, mm.n)] if s >= 2 else []
        p.dma("sync", xs[:], xnT_in[:, :, s * 512:(s + 1) * 512]
              .rearrange("k p t -> p k t"), ld, wait=w)
        ldv = ld.n
        for f in range(NF):
            pb = PH[f % 2]
            for k in range(8):
                waits = []
                if k == 0:
                    waits = [(ld, ldv)]
                    if s > 0 or f >= 2:
                        waits.append((ev, ev.n - 1))
                p.emit("tensor", lambda t, f=f, k=k, xs=xs, pb=pb:
                       t.matmul(pb[:], w1s[:, k, f * 128:(f + 1) * 128],
                                xs[:, k, :], start=(k == 0), stop=(k == 7)),
                       wait=waits, inc=((mm, 1),) if k == 7 else ())
            mv = mm.n
            p.emit("vector", lambda v, f=f, pb=pb: v.tensor_scalar(
                h1T[:, f, :], pb[:], b1s[:, f:f + 1], 0.0, ALU.add, ALU.max),
                wait=[(mm, mv)], inc=((ev, 1),))
        h1_done = ev.n
        for tt in range(4):
            pe = PE_[tt % 2]
            gt = s * 4 + tt
            for f in range(NF):
                waits = []
                if f == 0:
                    waits = [(ev, h1_done)]
                    if gt >= 2:
                        waits.append((ev, eo_evac_vals[gt - 2]))
                for half in range(2):
                    p.emit("tensor", lambda t, f=f, tt=tt, half=half, pe=pe:
                           t.matmul(pe[:, half * 512:(half + 1) * 512],
                                    h1T[:, f, tt * 128:(tt + 1) * 128],
                                    w2s[:, f, half * 512:(half + 1) * 512],
                                    start=(f == 0), stop=False),
                           wait=waits if (f == 0 and half == 0) else [])
            for half in range(2):
                p.emit("tensor", lambda t, half=half, pe=pe:
                       t.matmul(pe[:, half * 512:(half + 1) * 512], ones1[:],
                                b2s[:, half * 512:(half + 1) * 512],
                                start=False, stop=True),
                       inc=((mm, 1),) if half == 1 else ())
            mv = mm.n
            eb = eos[gt % 2]
            wv = [(mm, mv)]
            if gt >= 2:
                wv.append((sto, 16 * (gt - 1)))
            p.emit("scalar", lambda sc, eb=eb, pe=pe: sc.copy(eb[:], pe[:]),
                   wait=wv, inc=((ev, 1),))
            eo_evac_vals.append(ev.n)
            p.dma("sync", eo_out[gt * 128:(gt + 1) * 128, :], eb[:], sto,
                  wait=[(ev, ev.n)])
    p.wait("sync", sto, sto.n)
    p.finalize()
    return nc



def build_l0_gate():
    nc = new_nc()
    p = Prog(nc)
    TOK = TOK_PER_CORE
    NTILE = TOK // 128

    xo = nc.dram_tensor("xo", [TOK, D], F32, kind="ExternalInput").ap()
    eo = nc.dram_tensor("eo", [TOK, D], F32, kind="ExternalInput").ap()
    oT = nc.dram_tensor("oT", [D, TOK], F32, kind="ExternalInput").ap()
    wproj = nc.dram_tensor("wproj", [D, D], F32, kind="ExternalInput").ap()
    gamma = nc.dram_tensor("gamma", [1, D], F32, kind="ExternalInput").ap()
    beta = nc.dram_tensor("beta", [1, D], F32, kind="ExternalInput").ap()
    x1_out = nc.dram_tensor("x1", [TOK, D], F32, kind="ExternalOutput").ap()

    sb = nc.alloc_sbuf_tensor
    wps = sb("wps", [128, 8, D], BF16)
    gam = sb("gam", [1, D], F32)
    bet = sb("bet", [1, D], F32)
    ones1f = sb("ones1f", [1, 128], F32)
    gamb = sb("gamb", [128, D], F32)
    betb = sb("betb", [128, D], F32)
    oTs = [sb(f"oTs{i}", [128, 8, 128], BF16) for i in range(2)]
    eos = [sb(f"eos{i}", [128, D], F32) for i in range(2)]
    xx = [sb(f"xx{i}", [128, D], F32) for i in range(2)]
    ll = [sb(f"ll{i}", [128, D], F32) for i in range(2)]
    gg = [sb(f"gg{i}", [128, D], F32) for i in range(2)]
    x1s = [sb(f"x1s{i}", [128, D], F32) for i in range(2)]
    lnb = {
        "D": D,
        "x": [sb(f"lnx{i}", [128, D], F32) for i in range(2)],
        "xsq": [sb(f"lnq{i}", [128, D], F32) for i in range(2)],
        "st": [sb(f"lns{i}", [128, 8], F32) for i in range(2)],
        "xn": [sb(f"lnn{i}", [128, D], F32) for i in range(2)],
    }
    st2 = [sb(f"st2_{i}", [128, 8], F32) for i in range(2)]

    ps = nc.alloc_psum_tensor
    PS = [ps(f"PS{i}", [128, D], F32) for i in range(2)]
    PB = ps("PB", [128, D], F32)

    ld = p.sem("ld")
    gld = p.sem("gld")
    vch = p.sem("vch")
    sch = p.sem("sch")
    mm = p.sem("mm")
    ev = p.sem("ev")
    sto = p.sem("sto")
    gmc = p.sem("gmc")

    cv = setup_consts(p, [1e-5], gmc)
    p.wait("scalar", gmc, cv)
    p.dma("gpsimd", wps[:], wproj.rearrange("(k p) c -> p k c", p=128), gld)
    p.dma("sync", gam[:], gamma, ld)
    p.dma("sync", bet[:], beta, ld)
    p.emit("gpsimd", lambda g: g.memset(ones1f[:], 1.0), inc=((gmc, 1),))
    wld = ld.n
    onesv = gmc.n
    p.wait("tensor", gmc, onesv)
    p.wait("tensor", gld, 16)
    for half in range(2):
        p.emit("tensor", lambda t, half=half: t.matmul(
            PB[:, half * 512:(half + 1) * 512], ones1f[:],
            gam[:, half * 512:(half + 1) * 512], start=True, stop=True),
            wait=[(ld, wld)] if half == 0 else [], inc=((mm, 1),))
    p.emit("vector", lambda v: v.tensor_copy(gamb[:], PB[:]),
           wait=[(mm, mm.n)], inc=((vch, 1),))
    p.wait("tensor", vch, vch.n)
    for half in range(2):
        p.emit("tensor", lambda t, half=half: t.matmul(
            PB[:, half * 512:(half + 1) * 512], ones1f[:],
            bet[:, half * 512:(half + 1) * 512], start=True, stop=True),
            inc=((mm, 1),))
    p.emit("vector", lambda v: v.tensor_copy(betb[:], PB[:]),
           wait=[(mm, mm.n)], inc=((vch, 1),))
    gb_ready = (vch, vch.n)

    def load_tile(i, xbuf, waits):
        p.dma("sync", xbuf[:], xo[i * 128:(i + 1) * 128, :], ld, wait=waits)
        p.dma("sync", eos[i % 2][:], eo[i * 128:(i + 1) * 128, :], ld)
        ow = [(mm, mm.n)] if i >= 2 else []
        p.dma("gpsimd", oTs[i % 2][:],
              oT[:, i * 128:(i + 1) * 128].rearrange("(k p) t -> p k t", p=128),
              gld, wait=ow)

    def out_tile(i, xn_ap, ready):
        pb = i % 2
        ldv = ld.n
        gldv = gld.n
        pe = PS[pb]
        for k in range(8):
            waits = []
            if k == 0:
                waits = [(ld, ldv), (gld, gldv), (vch, vch.n)]
                if i >= 2:
                    waits.append((ev, ev.n - 1))
            for half in range(2):
                p.emit("tensor", lambda t, k=k, half=half, pb=pb, pe=pe:
                       t.matmul(pe[:, half * 512:(half + 1) * 512],
                                oTs[pb][:, k, :],
                                wps[:, k, half * 512:(half + 1) * 512],
                                start=(k == 0), stop=(k == 7)),
                       wait=waits if (k == 0 and half == 0) else [],
                       inc=((mm, 1),) if (k == 7 and half == 1) else ())
        smv = mm.n
        p.emit("vector", lambda v, pb=pb, xn_ap=xn_ap:
               v.tensor_tensor(xx[pb][:], xn_ap[:], eos[pb][:], ALU.add),
               wait=[ready], inc=((vch, 1),))
        p.emit("vector", lambda v, pb=pb: v.reduce_sum(
            st2[pb][:, 0:1], xx[pb][:], axis=AX.X),
            wait=[(vch, vch.n)], inc=((vch, 1),))
        p.emit("scalar", lambda s, pb=pb: s.activation(
            lnb["xsq"][pb][:], xx[pb][:], AF.Square, accum_out=st2[pb][:, 1:2]),
            wait=[(vch, vch.n - 1)], inc=((sch, 1),))
        p.emit("vector", lambda v, pb=pb: v.tensor_scalar(
            st2[pb][:, 2:3], st2[pb][:, 0:1], 1.0 / D, None, ALU.mult),
            wait=[(vch, vch.n)], inc=((vch, 1),))
        p.emit("vector", lambda v, pb=pb: v.tensor_scalar(
            st2[pb][:, 3:4], st2[pb][:, 1:2], 1.0 / D, None, ALU.mult),
            wait=[(sch, sch.n)], inc=((vch, 1),))
        p.emit("vector", lambda v, pb=pb: v.tensor_tensor(
            st2[pb][:, 4:5], st2[pb][:, 2:3], st2[pb][:, 2:3], ALU.mult),
            wait=[(vch, vch.n)], inc=((vch, 1),))
        p.emit("vector", lambda v, pb=pb: v.tensor_tensor(
            st2[pb][:, 5:6], st2[pb][:, 3:4], st2[pb][:, 4:5], ALU.subtract),
            wait=[(vch, vch.n)], inc=((vch, 1),))
        p.emit("scalar", lambda s, pb=pb: s.activation(
            st2[pb][:, 6:7], st2[pb][:, 5:6], AF.Sqrt, bias=1e-5),
            wait=[(vch, vch.n)], inc=((sch, 1),))
        p.emit("vector", lambda v, pb=pb: v.reciprocal(
            st2[pb][:, 7:8], st2[pb][:, 6:7]),
            wait=[(sch, sch.n)], inc=((vch, 1),))
        p.emit("vector", lambda v, pb=pb: v.tensor_scalar(
            ll[pb][:], xx[pb][:], st2[pb][:, 2:3], st2[pb][:, 7:8],
            ALU.subtract, ALU.mult),
            wait=[(vch, vch.n)], inc=((vch, 1),))
        p.emit("vector", lambda v, pb=pb: v.tensor_tensor(
            ll[pb][:], ll[pb][:], gamb[:], ALU.mult),
            wait=[(vch, vch.n), gb_ready], inc=((vch, 1),))
        p.emit("vector", lambda v, pb=pb: v.tensor_tensor(
            ll[pb][:], ll[pb][:], betb[:], ALU.add),
            wait=[(vch, vch.n)], inc=((vch, 1),))
        p.emit("scalar", lambda s, pb=pb: s.activation(
            gg[pb][:], ll[pb][:], AF.Silu),
            wait=[(vch, vch.n)], inc=((sch, 1),))
        gv = sch.n
        xw = [(sch, gv), (mm, smv)]
        if i >= 2:
            xw.append((sto, 16 * (i - 1)))
        p.emit("vector", lambda v, pb=pb, pe=pe: v.tensor_tensor(
            x1s[pb][:], pe[:], gg[pb][:], ALU.mult),
            wait=xw, inc=((vch, 1),))
        p.emit("vector", lambda v, pb=pb, xbuf=lnb["x"][pb]: v.tensor_tensor(
            x1s[pb][:], x1s[pb][:], xbuf[:], ALU.add),
            wait=[(vch, vch.n)], inc=((ev, 1),))
        p.dma("sync", x1_out[i * 128:(i + 1) * 128, :], x1s[pb][:], sto,
              wait=[(ev, ev.n)])
        return (sto, sto.n)

    emit_ln_tiles(p, NTILE, load_tile, out_tile, lnb,
                  dict(ld=ld, vch=vch, sch=sch))
    p.wait("sync", sto, sto.n)
    p.finalize()
    return nc



def build_l1_attn(debug=False):
    nc = new_nc()
    p = Prog(nc)
    NCH = T // 128
    NSL = T // 512

    x1b = nc.dram_tensor("x1b", [T, D], F32, kind="ExternalInput").ap()
    wq = nc.dram_tensor("wq", [D, 2, 256], F32, kind="ExternalInput").ap()
    wk = nc.dram_tensor("wk", [D, 2, 256], F32, kind="ExternalInput").ap()
    wv = nc.dram_tensor("wv", [D, 2, 256], F32, kind="ExternalInput").ap()
    wf = nc.dram_tensor("wf", [D, 2, 64], F32, kind="ExternalInput").ap()
    gnw = nc.dram_tensor("gnw", [1, 256], F32, kind="ExternalInput").ap()
    ident = nc.dram_tensor("ident", [128, 128], BF16, kind="ExternalInput").ap()
    identf = nc.dram_tensor("identf", [128, 128], F32, kind="ExternalInput").ap()
    triu = nc.dram_tensor("triu", [128, 128], BF16, kind="ExternalInput").ap()
    r127 = nc.dram_tensor("r127", [128, 128], F32, kind="ExternalInput").ap()
    on_out = nc.dram_tensor("on", [T, 2, 256], F32, kind="ExternalOutput").ap()
    if debug:
        dbg_stbf = nc.dram_tensor("dbg_stbf", [64, T], F32, kind="ExternalOutput").ap()
        dbg_stok = nc.dram_tensor("dbg_stok", [128, NCH, 64], F32, kind="ExternalOutput").ap()
        dbg_ttok = nc.dram_tensor("dbg_ttok", [128, NCH, 64], F32, kind="ExternalOutput").ap()
        dbg_ab = nc.dram_tensor("dbg_ab", [4, 128, 128], F32, kind="ExternalOutput").ap()
        dbg_lg = nc.dram_tensor("dbg_lg", [4, 128, 64], F32, kind="ExternalOutput").ap()
        dbg_eh = nc.dram_tensor("dbg_eh", [4, 128, 64], F32, kind="ExternalOutput").ap()
        dbg_wb = nc.dram_tensor("dbg_wb", [4, 128, 128], F32, kind="ExternalOutput").ap()
        dbg_etb = nc.dram_tensor("dbg_etb", [4, 64, 128], F32, kind="ExternalOutput").ap()
        dbg_pw = nc.dram_tensor("dbg_pw", [4, 128, 128], F32, kind="ExternalOutput").ap()
        pass

    sb = nc.alloc_sbuf_tensor
    xnT = sb("xnT", [128, 8, T], BF16)
    QT = sb("QT", [128, 2, T], BF16)
    KT = sb("KT", [128, 2, T], BF16)
    VTs = sb("VTs", [128, 2, 512], BF16)
    Ktok = sb("Ktok", [128, NCH, 256], BF16)
    Vtok = sb("Vtok", [128, NCH, 256], BF16)
    gpk = sb("gpk", [64, 2048], F32)
    ga = gpk[:, 0:512]
    gb = gpk[:, 512:1024]
    gc = gpk[:, 1024:1536]
    gd = gpk[:, 1536:2048]
    tLc = sb("tLc", [64, NCH], F32)
    STbf = sb("STbf", [64, T], BF16)
    Stok = sb("Stok", [128, NCH, 64], BF16)
    Ttok = sb("Ttok", [128, NCH, 64], F32)
    zeros64 = sb("zeros64", [64, 128], F32)
    hkf = sb("hkf", [128, 128], F32)
    hkb = sb("hkb", [128, 128], BF16)
    hvf = sb("hvf", [64, 256], F32)
    hvb = sb("hvb", [64, 256], BF16)
    ostg = sb("ostg", [128, 256], F32)
    stt = sb("stt", [128, 8], F32)

    gnws = sb("gnws", [1, 256], F32)
    gnwb = sb("gnwb", [128, 256], F32)
    ones1f = sb("ones1f", [1, 128], F32)
    ident_s = sb("ident_s", [128, 128], BF16)
    identf_s = sb("identf_s", [64, 64], F32)
    triu_s = sb("triu_s", [128, 128], BF16)
    r127_s = sb("r127_s", [128, 128], F32)
    wq_s = sb("wq_s", [128, 8, 2, 256], BF16)
    wk_s = sb("wk_s", [128, 8, 2, 256], BF16)
    wv_s = sb("wv_s", [128, 8, 2, 256], BF16)
    wf_s = sb("wf_s", [128, 8, 2, 64], BF16)
    lnb = {
        "D": D,
        "x": [sb(f"lnx{i}", [128, D], F32) for i in range(2)],
        "xsq": [sb("lnq0", [128, D], F32)] * 2,
        "st": [sb(f"lns{i}", [128, 8], F32) for i in range(2)],
        "xn": [sb(f"lnn{i}", [128, D], BF16) for i in range(2)],
    }

    dbgb = lnb["xsq"][0][:, 512:640]
    hkt = lnb["xsq"][0][:, 768:896]
    hvt = lnb["xsq"][0][0:64, 0:256]
    Ebuf = lnb["xsq"][0][:, 896:960]
    lg = lnb["xsq"][0][:, 960:1024]
    Abf = VTs[:, 0, 0:128]
    Wbf = VTs[:, 0, 128:256]
    Ehb = VTs[:, 0, 256:320]
    ETb = VTs[0:64, 0, 320:448]

    ps = nc.alloc_psum_tensor
    PP = [ps(f"PP{i}", [128, 512], F32) for i in range(2)]
    PAK = ps("PAK", [128, 256], F32)
    PW = ps("PW", [128, 128], F32)
    PCV = ps("PCV", [128, 448], F32)
    PO = ps("PO", [128, 256], F32)
    PT = ps("PT", [128, 2048], BF16)

    ld = p.sem("ld")
    gld = p.sem("gld")
    vch = p.sem("vch")
    sch = p.sem("sch")
    mm = p.sem("mm")
    tr = p.sem("tr")
    ev = p.sem("ev")
    sto = p.sem("sto")
    gmc = p.sem("gmc")

    cv = setup_consts(p, [-1.0, 1.0, 1e-5], gmc)
    p.wait("scalar", gmc, cv)

    p.dma("sync", ident_s[:], ident, ld)
    p.dma("sync", identf_s[:], identf[0:64, 0:64], ld)
    p.dma("sync", triu_s[:], triu, ld)
    p.dma("sync", r127_s[:], r127, ld)
    p.dma("sync", gnws[:], gnw, ld)
    p.dma("gpsimd", wq_s[:], wq.rearrange("(k p) h c -> p k h c", p=128), gld)
    p.dma("gpsimd", wk_s[:], wk.rearrange("(k p) h c -> p k h c", p=128), gld)
    p.dma("gpsimd", wv_s[:], wv.rearrange("(k p) h c -> p k h c", p=128), gld)
    p.dma("gpsimd", wf_s[:], wf.rearrange("(k p) h c -> p k h c", p=128), gld)
    p.emit("gpsimd", lambda g: g.memset(zeros64[:], 0.0), inc=((gmc, 1),))
    p.emit("gpsimd", lambda g: g.memset(ones1f[:], 1.0), inc=((gmc, 1),))
    consts_ld = ld.n
    wld_v = gld.n
    memv = gmc.n
    p.wait("tensor", gld, wld_v)
    p.wait("tensor", gmc, memv)
    p.wait("vector", gld, wld_v)
    p.wait("vector", gmc, memv)

    p.emit("tensor", lambda t: t.matmul(PO[:, 0:256], ones1f[:], gnws[:],
                                        start=True, stop=True),
           wait=[(ld, consts_ld)], inc=((mm, 1),))
    p.emit("vector", lambda v: v.tensor_copy(gnwb[:], PO[:, 0:256]),
           wait=[(mm, mm.n)], inc=((vch, 1),))
    gnw_ready = (vch, vch.n)

    def load_tile(i, xbuf, waits):
        p.dma("sync", xbuf[:], x1b[i * 128:(i + 1) * 128, :], ld, wait=waits)

    def out_tile(i, xn_ap, ready):
        for k in range(8):
            half = k % 2
            w = [ready, (ld, consts_ld)]
            if i > 0 or k >= 2:
                w.append((tr, tr.n - 1))
            p.emit("tensor", lambda t, xn_ap=xn_ap, k=k, half=half:
                   t.transpose(PT[:, half * 1024:half * 1024 + 128],
                               xn_ap[:, k * 128:(k + 1) * 128], ident_s[:]),
                   wait=w, inc=((mm, 1),))
            p.emit("scalar", lambda s, i=i, k=k, half=half:
                   s.copy(xnT[:, k, i * 128:(i + 1) * 128],
                          PT[:, half * 1024:half * 1024 + 128]),
                   wait=[(mm, mm.n)], inc=((tr, 1),))
        return (tr, tr.n)

    emit_ln_tiles(p, NCH, load_tile, out_tile, lnb,
                  dict(ld=ld, vch=vch, sch=sch))
    xnT_ready = (tr, tr.n)

    def chain(eng, fn, extra=(), engsem=None):
        s_ = engsem or (vch if eng == "vector" else sch)
        o_ = sch if s_ is vch else vch
        p.emit(eng, fn, wait=[(vch, vch.n), (sch, sch.n)] + list(extra),
               inc=((s_, 1),))

    for hl in range(2):
        zw = [(vch, vch.n)] if hl > 0 else []
        p.emit("gpsimd", lambda g: g.memset(hkf[:], 0.0), wait=zw,
               inc=((gmc, 1),))
        p.emit("gpsimd", lambda g: g.memset(hkb[:], 0.0), inc=((gmc, 1),))
        p.emit("gpsimd", lambda g: g.memset(hvf[:], 0.0), inc=((gmc, 1),))
        p.emit("gpsimd", lambda g: g.memset(hvb[:], 0.0), inc=((gmc, 1),))
        zs_v = gmc.n

        for sl in range(NSL):
            for kc in range(2):
                for which, wsrc, dst in (("q", wq_s, QT), ("k", wk_s, KT),
                                         ("v", wv_s, None)):
                    pp = PP[0]
                    waits = [xnT_ready, (ld, consts_ld), (ev, ev.n)]
                    for k in range(8):
                        p.emit("tensor", lambda t, k=k, sl=sl, kc=kc, hl=hl,
                               wsrc=wsrc, pp=pp:
                               t.matmul(pp[:], wsrc[:, k, hl,
                                                    kc * 128:(kc + 1) * 128]
                                        if wsrc is not wf_s else None,
                                        xnT[:, k, sl * 512:(sl + 1) * 512],
                                        start=(k == 0), stop=(k == 7)),
                               wait=waits if k == 0 else [],
                               inc=((mm, 1),) if k == 7 else ())
                    mv = mm.n
                    if which == "q":
                        p.emit("scalar", lambda s, pp=pp: s.activation(
                            lnb["xsq"][0][:, 0:512], pp[:], AF.Silu),
                            wait=[(mm, mv)], inc=((sch, 1),))
                        p.emit("vector", lambda v, sl=sl, kc=kc: v.tensor_scalar(
                            QT[:, kc, sl * 512:(sl + 1) * 512], lnb["xsq"][0][:, 0:512],
                            1.0 / 16.0, None, ALU.mult),
                            wait=[(sch, sch.n)], inc=((ev, 1),))
                    elif which == "k":
                        p.emit("scalar", lambda s, sl=sl, kc=kc, pp=pp:
                               s.activation(KT[:, kc, sl * 512:(sl + 1) * 512],
                                            pp[:], AF.Silu),
                               wait=[(mm, mv)], inc=((ev, 1),))
                    else:
                        p.emit("vector", lambda v, kc=kc, pp=pp: v.tensor_copy(
                            VTs[:, kc, :], pp[:]),
                            wait=[(mm, mv)], inc=((ev, 1),))
            for tb in range(4):
                ch_ = sl * 4 + tb
                for kc in range(2):
                    half = (tb * 2 + kc) % 2
                    w = [(ev, ev.n)]
                    if sl > 0 or tb > 0 or kc > 0:
                        w.append((tr, tr.n - 1))
                    p.emit("tensor", lambda t, tb=tb, kc=kc, half=half:
                           t.transpose(PT[:, half * 1024:half * 1024 + 128],
                                       VTs[:, kc, tb * 128:(tb + 1) * 128],
                                       ident_s[:]),
                           wait=w, inc=((mm, 1),))
                    p.emit("scalar", lambda s, ch_=ch_, kc=kc, half=half:
                           s.copy(Vtok[:, ch_, kc * 128:(kc + 1) * 128],
                                  PT[:, half * 1024:half * 1024 + 128]),
                           wait=[(mm, mm.n)], inc=((tr, 1),))
        for ch_ in range(NCH):
            for kc in range(2):
                half = (ch_ * 2 + kc) % 2
                w = [(ev, ev.n), (tr, tr.n - 1)]
                p.emit("tensor", lambda t, ch_=ch_, kc=kc, half=half:
                       t.transpose(PT[:, half * 1024:half * 1024 + 128],
                                   KT[:, kc, ch_ * 128:(ch_ + 1) * 128],
                                   ident_s[:]),
                       wait=w, inc=((mm, 1),))
                p.emit("scalar", lambda s, ch_=ch_, kc=kc, half=half:
                       s.copy(Ktok[:, ch_, kc * 128:(kc + 1) * 128],
                              PT[:, half * 1024:half * 1024 + 128]),
                       wait=[(mm, mm.n)], inc=((tr, 1),))

        for sl in range(NSL):
            pp = PP[1]
            waits = [xnT_ready, (ld, consts_ld), (vch, vch.n), (sch, sch.n)]
            for k in range(8):
                p.emit("tensor", lambda t, k=k, sl=sl, hl=hl, pp=pp:
                       t.matmul(pp[0:64, :], wf_s[:, k, hl, :],
                                xnT[:, k, sl * 512:(sl + 1) * 512],
                                start=(k == 0), stop=(k == 7)),
                       wait=waits if k == 0 else [],
                       inc=((mm, 1),) if k == 7 else ())
            p.emit("vector", lambda v, pp=pp: v.tensor_copy(ga, pp[0:64, :]),
                   wait=[(mm, mm.n)], inc=((vch, 1),))
            chain("scalar", lambda s: s.activation(gb, ga, AF.Abs))
            chain("scalar", lambda s: s.activation(gb, gb, AF.Exp,
                                                   scale=-1.0))
            chain("scalar", lambda s: s.activation(gb, gb, AF.Ln,
                                                   bias=1.0))
            chain("vector", lambda v: v.tensor_scalar(gd, ga, 0.0, None,
                                                      ALU.min))
            chain("vector", lambda v: v.tensor_tensor(gd, gd, gb,
                                                      ALU.subtract))
            chain("vector", lambda v: v.tensor_scalar(ga, gd, 0.125,
                                                      None, ALU.mult))
            chain("scalar", lambda s: s.activation(gd, ga, AF.Exp))
            chain("vector", lambda v: v.tensor_scalar(gd, gd, -1.0, 1.0,
                                                      ALU.mult, ALU.add))
            for cb in range(4):
                chain("vector", lambda v, cb=cb: v.tensor_tensor_scan(
                    gb[:, cb * 128:(cb + 1) * 128],
                    ga[:, cb * 128:(cb + 1) * 128], zeros64[:], 0.0,
                    ALU.add, ALU.add))
            chain("scalar", lambda s: s.activation(gc, gb, AF.Exp))
            chain("scalar", lambda s: s.activation(ga, gb, AF.Exp,
                                                   scale=-1.0))
            chain("vector", lambda v: v.tensor_mul(gb, gd, ga))
            chain("vector", lambda v, sl=sl: v.tensor_copy(
                STbf[:, sl * 512:(sl + 1) * 512], gb))
            for cb in range(4):
                ch_ = sl * 4 + cb
                chain("vector", lambda v, cb=cb, ch_=ch_: v.tensor_copy(
                    tLc[:, ch_:ch_ + 1], gc[:, cb * 128 + 127:cb * 128 + 128]))
            gate_v = vch.n
            gate_s = sch.n
            for cb in range(4):
                ch_ = sl * 4 + cb
                half = cb % 2
                w = [(vch, gate_v), (sch, gate_s)]
                if sl > 0 or cb >= 2:
                    w.append((tr, tr.n - 1))
                p.emit("tensor", lambda t, ch_=ch_, cb=cb, half=half:
                       t.transpose(PT[:, half * 1024:half * 1024 + 64],
                                   STbf[:, ch_ * 128:(ch_ + 1) * 128],
                                   ident_s[0:64, 0:64]),
                       wait=w, inc=((mm, 1),))
                p.emit("scalar", lambda s, ch_=ch_, half=half:
                       s.copy(Stok[:, ch_, :],
                              PT[:, half * 1024:half * 1024 + 64]),
                       wait=[(mm, mm.n)], inc=((tr, 1),))
                p.emit("tensor", lambda t, cb=cb:
                       t.transpose(PW[:, 0:64], gc[:, cb * 128:(cb + 1) * 128],
                                   identf_s[:]),
                       wait=[(tr, tr.n)], inc=((mm, 1),))
                p.emit("vector", lambda v, ch_=ch_: v.tensor_copy(
                    Ttok[:, ch_, :], PW[:, 0:64]),
                    wait=[(mm, mm.n)], inc=((vch, 1),))
                p.wait("tensor", vch, vch.n)

        if debug and hl == 0:
            p.dma("gpsimd", dbg_stbf, STbf[:], gld,
                  wait=[(vch, vch.n), (sch, sch.n), (tr, tr.n)])
            p.dma("gpsimd", dbg_stok, Stok[:], gld,
                  wait=[(vch, vch.n), (sch, sch.n), (tr, tr.n)])
            p.dma("sync", dbg_ttok, Ttok[:], sto,
                  wait=[(vch, vch.n), (sch, sch.n), (tr, tr.n)])
            p.wait("vector", sto, sto.n)
            p.wait("vector", gld, gld.n)
        p.wait("tensor", gmc, zs_v)
        p.wait("vector", gmc, zs_v)
        for ch_ in range(NCH):
            csl = slice(ch_ * 128, (ch_ + 1) * 128)
            for kc in range(2):
                p.emit("tensor", lambda t, kc=kc, csl=csl:
                       t.matmul(PAK[:, 0:128], KT[:, kc, csl], QT[:, kc, csl],
                                start=(kc == 0), stop=(kc == 1)),
                       wait=[(vch, vch.n), (sch, sch.n)] if kc == 0 else [],
                       inc=((mm, 1),) if kc == 1 else ())
            p.emit("vector", lambda v: v.tensor_tensor(Abf, PAK[:, 0:128],
                                                       triu_s[:], ALU.mult),
                   wait=[(mm, mm.n)], inc=((vch, 1),))
            abf_v = vch.n
            if debug and hl == 0 and ch_ < 4:
                p.emit("vector", lambda v: v.tensor_copy(dbgb[:], Abf),
                       wait=[(vch, abf_v)], inc=((vch, 1),))
                p.dma("sync", dbg_ab[ch_], dbgb[:], sto, wait=[(vch, vch.n)])
                p.wait("vector", sto, sto.n)
            p.emit("tensor", lambda t, ch_=ch_: t.matmul(
                PCV[:, 0:64], Abf, Stok[:, ch_, :], start=True, stop=False),
                wait=[(vch, abf_v)])
            for kc in range(2):
                p.emit("tensor", lambda t, kc=kc, csl=csl: t.matmul(
                    PCV[:, 0:64], QT[:, kc, csl], hkb[:, kc * 64:(kc + 1) * 64],
                    start=False, stop=(kc == 1)),
                    inc=((mm, 1),) if kc == 1 else ())
            p.emit("vector", lambda v, ch_=ch_: v.tensor_tensor(
                lg, PCV[:, 0:64], Ttok[:, ch_, :], ALU.mult),
                wait=[(mm, mm.n)], inc=((vch, 1),))
            if debug and hl == 0 and ch_ < 4:
                p.dma("sync", dbg_lg[ch_], lg, sto, wait=[(vch, vch.n)])
                p.wait("vector", sto, sto.n)
            p.emit("vector", lambda v: v.reduce_max(stt[:, 0:1], lg,
                                                    axis=AX.X),
                   wait=[(vch, vch.n)], inc=((vch, 1),))
            p.emit("vector", lambda v: v.tensor_scalar(
                lg, lg, stt[:, 0:1], None, ALU.subtract),
                wait=[(vch, vch.n)], inc=((vch, 1),))
            p.emit("scalar", lambda s: s.activation(
                Ebuf, lg, AF.Exp, accum_out=stt[:, 1:2]),
                wait=[(vch, vch.n)], inc=((sch, 1),))
            p.emit("vector", lambda v: v.reciprocal(stt[:, 2:3], stt[:, 1:2]),
                   wait=[(sch, sch.n)], inc=((vch, 1),))
            p.emit("vector", lambda v, ch_=ch_: v.tensor_tensor(
                Ehb, Ebuf, Ttok[:, ch_, :], ALU.mult),
                wait=[(vch, vch.n - 1), (sch, sch.n)], inc=((vch, 1),))
            ehb_v = vch.n
            if debug and hl == 0 and ch_ < 4:
                p.emit("vector", lambda v: v.tensor_copy(dbgb[:, 0:64], Ehb),
                       wait=[(vch, ehb_v)], inc=((vch, 1),))
                p.dma("sync", dbg_eh[ch_], dbgb[:, 0:64], sto,
                      wait=[(vch, vch.n)])
                p.wait("vector", sto, sto.n)
            p.emit("tensor", lambda t: t.transpose(PT[0:64, 0:128], Ehb,
                                                   ident_s[:]),
                   wait=[(vch, ehb_v), (tr, tr.n)], inc=((mm, 1),))
            p.emit("scalar", lambda s: s.copy(ETb, PT[0:64, 0:128]),
                   wait=[(mm, mm.n)], inc=((tr, 1),))
            etb_t = tr.n
            if debug and hl == 0 and ch_ < 4:
                p.emit("vector", lambda v: v.tensor_copy(dbgb[0:64, :], ETb),
                       wait=[(tr, etb_t)], inc=((vch, 1),))
                p.dma("sync", dbg_etb[ch_], dbgb[0:64, :], sto,
                      wait=[(vch, vch.n)])
                p.wait("vector", sto, sto.n)
                p.wait("scalar", sto, sto.n)
            p.emit("tensor", lambda t, csl=csl: t.matmul(
                PW[:], STbf[:, csl.start:csl.stop], ETb,
                start=True, stop=True),
                wait=[(tr, etb_t)], inc=((mm, 1),))
            if debug and hl == 0 and ch_ < 4:
                p.emit("vector", lambda v: v.tensor_copy(dbgb[:], PW[:]),
                       wait=[(mm, mm.n)], inc=((vch, 1),))
                p.dma("sync", dbg_pw[ch_], dbgb[:], sto, wait=[(vch, vch.n)])
                p.wait("vector", sto, sto.n)
            p.emit("vector", lambda v: v.tensor_tensor(Wbf, PW[:],
                                                       triu_s[:], ALU.mult),
                   wait=[(mm, mm.n)], inc=((vch, 1),))
            wbf_v = vch.n
            if debug and hl == 0 and ch_ < 4:
                p.emit("vector", lambda v: v.tensor_copy(dbgb[:], Wbf),
                       wait=[(vch, wbf_v)], inc=((vch, 1),))
                p.dma("sync", dbg_wb[ch_], dbgb[:], sto, wait=[(vch, vch.n)])
                p.wait("vector", sto, sto.n)
            p.emit("tensor", lambda t, ch_=ch_: t.matmul(
                PO[:], Wbf, Vtok[:, ch_, :], start=True, stop=False),
                wait=[(vch, wbf_v)])
            p.emit("tensor", lambda t: t.matmul(
                PO[:], ETb, hvb[:], start=False, stop=True),
                inc=((mm, 1),))
            o_mm = mm.n
            p.emit("scalar", lambda s: s.activation(
                lnb["xsq"][0][:, 512:768], PO[:], AF.Square, accum_out=stt[:, 3:4]),
                wait=[(mm, o_mm)], inc=((sch, 1),))
            p.emit("vector", lambda v: v.tensor_tensor(
                stt[:, 4:5], stt[:, 2:3], stt[:, 2:3], ALU.mult),
                wait=[(sch, sch.n)], inc=((vch, 1),))
            p.emit("vector", lambda v: v.tensor_tensor(
                stt[:, 4:5], stt[:, 3:4], stt[:, 4:5], ALU.mult),
                wait=[(vch, vch.n)], inc=((vch, 1),))
            p.emit("vector", lambda v: v.tensor_scalar(
                stt[:, 4:5], stt[:, 4:5], 1.0 / 256.0, None, ALU.mult),
                wait=[(vch, vch.n)], inc=((vch, 1),))
            p.emit("scalar", lambda s: s.activation(
                stt[:, 5:6], stt[:, 4:5], AF.Sqrt, bias=1e-5),
                wait=[(vch, vch.n)], inc=((sch, 1),))
            p.emit("vector", lambda v: v.reciprocal(stt[:, 6:7], stt[:, 5:6]),
                   wait=[(sch, sch.n)], inc=((vch, 1),))
            p.emit("vector", lambda v: v.tensor_tensor(
                stt[:, 7:8], stt[:, 6:7], stt[:, 2:3], ALU.mult),
                wait=[(vch, vch.n)], inc=((vch, 1),))
            p.emit("vector", lambda v: v.tensor_scalar(
                ostg[:], PO[:], stt[:, 7:8], None, ALU.mult),
                wait=[(vch, vch.n)], inc=((vch, 1),))
            p.emit("vector", lambda v: v.tensor_tensor(
                ostg[:], ostg[:], gnwb[:], ALU.mult),
                wait=[(vch, vch.n), gnw_ready], inc=((ev, 1),))
            ow = [(ev, ev.n)]
            if ch_ >= 2:
                ow.append((sto, sto.n - 16))
            p.dma("sync", on_out[csl, hl, :], ostg[:], sto, wait=ow)
            for kc in range(2):
                p.emit("tensor", lambda t, kc=kc, ch_=ch_: t.matmul(
                    PAK[:, 128 + kc * 64:128 + (kc + 1) * 64],
                    Ktok[:, ch_, kc * 128:(kc + 1) * 128], Stok[:, ch_, :],
                    start=True, stop=True),
                    wait=[(sch, sch.n)] if kc == 0 else [],
                    inc=((mm, 1),) if kc == 1 else ())
            hk_mm = mm.n
            p.emit("tensor", lambda t, ch_=ch_: t.matmul(
                PCV[0:64, 192:448], Stok[:, ch_, :], Vtok[:, ch_, :],
                start=True, stop=True), inc=((mm, 1),))
            hv_mm = mm.n
            p.emit("tensor", lambda t, ch_=ch_: t.matmul(
                PCV[:, 64:128], r127_s[:], Ttok[:, ch_, :],
                start=True, stop=True), inc=((mm, 1),))
            tl_mm = mm.n
            p.emit("vector", lambda v: v.tensor_tensor(
                hkt, hkf[:], PAK[:, 128:256], ALU.add),
                wait=[(mm, hk_mm)], inc=((vch, 1),))
            p.emit("vector", lambda v: v.tensor_tensor(
                hkf[:, 0:64], hkt[:, 0:64], PCV[:, 64:128], ALU.mult),
                wait=[(vch, vch.n), (mm, tl_mm)], inc=((vch, 1),))
            p.emit("vector", lambda v: v.tensor_tensor(
                hkf[:, 64:128], hkt[:, 64:128], PCV[:, 64:128], ALU.mult),
                wait=[(vch, vch.n)], inc=((vch, 1),))
            p.emit("vector", lambda v: v.tensor_copy(hkb[:], hkf[:]),
                   wait=[(vch, vch.n)], inc=((vch, 1),))
            hkb_v = vch.n
            p.emit("vector", lambda v: v.tensor_tensor(
                hvt, hvf[:], PCV[0:64, 192:448], ALU.add),
                wait=[(mm, hv_mm)], inc=((vch, 1),))
            p.emit("vector", lambda v, ch_=ch_: v.tensor_scalar(
                hvf[:], hvt, tLc[:, ch_:ch_ + 1], None, ALU.mult),
                wait=[(vch, vch.n)], inc=((vch, 1),))
            p.emit("vector", lambda v: v.tensor_copy(hvb[:], hvf[:]),
                   wait=[(vch, vch.n)], inc=((vch, 1),))
            p.wait("tensor", vch, vch.n)
    p.wait("sync", sto, sto.n)
    p.finalize()
    return nc



def build_l1_router(debug=False):
    nc = new_nc()
    p = Prog(nc)
    TOK = TOK_PER_CORE
    NTILE = TOK // 128

    x1 = nc.dram_tensor("x1", [TOK, D], F32, kind="ExternalInput").ap()
    oT1 = nc.dram_tensor("oT1", [D, TOK], F32, kind="ExternalInput").ap()
    wo = nc.dram_tensor("wo", [D, D], F32, kind="ExternalInput").ap()
    rw = nc.dram_tensor("rw", [D, 8], F32, kind="ExternalInput").ap()
    rb = nc.dram_tensor("rb", [1, 8], F32, kind="ExternalInput").ap()
    ident = nc.dram_tensor("ident", [128, 128], BF16, kind="ExternalInput").ap()
    identf = nc.dram_tensor("identf", [128, 128], F32, kind="ExternalInput").ap()
    s1_out = nc.dram_tensor("s1", [TOK, D], F32, kind="ExternalOutput").ap()
    wexp_out = nc.dram_tensor("wexp", [TOK, 8], F32, kind="ExternalOutput").ap()
    xn1T_out = nc.dram_tensor("xn1T", [8, 128, TOK], BF16,
                              kind="ExternalOutput").ap()
    if debug:
        dbg_r = nc.dram_tensor("dbg_r", [TOK, 32], F32,
                               kind="ExternalOutput").ap()

    sb = nc.alloc_sbuf_tensor
    wos = sb("wos", [128, 8, D], BF16)
    rws = sb("rws", [128, 8, 8], F32)
    rbs = sb("rbs", [1, 8], F32)
    ones1f = sb("ones1f", [1, 128], F32)
    ident_s = sb("ident_s", [128, 128], BF16)
    identf_s = sb("identf_s", [128, 128], F32)
    oTs = [sb(f"oTs{i}", [128, 8, 128], BF16) for i in range(2)]
    xnTf = [sb(f"xnTf{i}", [128, 8, 128], F32) for i in range(2)]
    xnTb = [sb(f"xnTb{i}", [128, 8, 128], BF16) for i in range(2)]
    s1s = [sb(f"s1s{i}", [128, D], F32) for i in range(2)]
    rr = [sb(f"rr{i}", [128, 32], F32) for i in range(2)]
    iota8 = sb("iota8", [128, 8], F32)
    lnb = {
        "D": D,
        "x": [sb(f"lnx{i}", [128, D], F32) for i in range(2)],
        "xsq": [sb("lnq0", [128, D], F32)] * 2,
        "st": [sb(f"lns{i}", [128, 8], F32) for i in range(2)],
        "xn": [sb(f"lnn{i}", [128, D], F32) for i in range(2)],
    }

    ps = nc.alloc_psum_tensor
    PS = [ps(f"PS{i}", [128, D], F32) for i in range(2)]
    PTF = ps("PTF", [128, 1024], F32)
    PR = ps("PR", [128, 8], F32)

    ld = p.sem("ld")
    gld = p.sem("gld")
    gmc = p.sem("gmc")
    vch = p.sem("vch")
    sch = p.sem("sch")
    mm = p.sem("mm")
    tr = p.sem("tr")
    ev = p.sem("ev")
    sto = p.sem("sto")

    cv = setup_consts(p, [1e-5], gmc)
    p.wait("scalar", gmc, cv)
    p.dma("gpsimd", wos[:], wo.rearrange("(k p) c -> p k c", p=128), gld)
    p.dma("sync", rws[:], rw.rearrange("(k p) e -> p k e", p=128), ld)
    p.dma("sync", rbs[:], rb, ld)
    p.dma("sync", ident_s[:], ident, ld)
    p.dma("sync", identf_s[:], identf, ld)
    p.emit("gpsimd", lambda g: g.memset(ones1f[:], 1.0), inc=((gmc, 1),))
    p.emit("gpsimd", lambda g: g.iota(
        iota8[:], [[1, 8]], base=10000, channel_multiplier=0,
        allow_small_or_imprecise_dtypes=True), inc=((gmc, 1),))
    consts_ld = ld.n
    memv = gmc.n
    p.wait("tensor", gld, 16)
    p.wait("tensor", gmc, memv)
    p.wait("vector", gmc, memv)

    def load_tile(i, xbuf, waits):
        p.dma("sync", xbuf[:], x1[i * 128:(i + 1) * 128, :], ld, wait=waits)
        ow = [(mm, mm.n)] if i >= 2 else []
        p.dma("gpsimd", oTs[i % 2][:],
              oT1[:, i * 128:(i + 1) * 128].rearrange("(k p) t -> p k t", p=128),
              gld, wait=ow)

    def out_tile(i, xn_ap, ready):
        pb = i % 2
        ldv = ld.n
        gldv = gld.n
        r = rr[pb]
        for k in range(8):
            half = k % 2
            w = [ready, (ld, consts_ld)]
            if i > 0 or k >= 2:
                w.append((tr, tr.n - 2))
            p.emit("tensor", lambda t, xn_ap=xn_ap, k=k, half=half:
                   t.transpose(PTF[:, half * 512:half * 512 + 128],
                               xn_ap[:, k * 128:(k + 1) * 128], identf_s[:]),
                   wait=w, inc=((mm, 1),))
            mv = mm.n
            p.emit("scalar", lambda s, k=k, pb=pb, half=half:
                   s.copy(xnTf[pb][:, k, :], PTF[:, half * 512:half * 512 + 128]),
                   wait=[(mm, mv)], inc=((tr, 1),))
            p.emit("vector", lambda v, k=k, pb=pb:
                   v.tensor_copy(xnTb[pb][:, k, :], xnTf[pb][:, k, :]),
                   wait=[(tr, tr.n)], inc=((tr, 1),))
        xnt_done = tr.n
        p.dma("sync", xn1T_out[:, :, i * 128:(i + 1) * 128]
              .rearrange("k p t -> p k t"), xnTb[pb][:], sto,
              wait=[(tr, xnt_done)])
        for k in range(8):
            p.emit("tensor", lambda t, k=k, pb=pb: t.matmul(
                PR[:], xnTf[pb][:, k, :], rws[:, k, :],
                start=(k == 0), stop=False),
                wait=[(tr, xnt_done), (vch, vch.n)] if k == 0 else [])
        p.emit("tensor", lambda t: t.matmul(
            PR[:], ones1f[:], rbs[:], start=False, stop=True),
            inc=((mm, 1),))
        rmv = mm.n
        p.emit("vector", lambda v, r=r: v.tensor_scalar(
            r[:, 0:8], PR[:], -50.0, 50.0, ALU.max, ALU.min),
            wait=[(mm, rmv)], inc=((vch, 1),))
        p.emit("scalar", lambda s, r=r: s.activation(
            r[:, 8:16], r[:, 0:8], AF.Sigmoid),
            wait=[(vch, vch.n)], inc=((sch, 1),))
        p.emit("vector", lambda v, r=r: v.reduce_max(
            r[:, 16:17], r[:, 8:16], axis=AX.X),
            wait=[(sch, sch.n)], inc=((vch, 1),))
        p.emit("vector", lambda v, r=r: v.tensor_tensor(
            r[:, 24:32], r[:, 8:16],
            r[:, 16:17].broadcast_to([128, 8]), ALU.subtract),
            wait=[(vch, vch.n)], inc=((vch, 1),))
        p.emit("vector", lambda v, r=r: v.tensor_scalar(
            r[:, 0:8], r[:, 24:32], 0.0, None, ALU.is_ge),
            wait=[(vch, vch.n)], inc=((vch, 1),))
        p.emit("vector", lambda v, r=r: v.scalar_tensor_tensor(
            r[:, 24:32], r[:, 0:8], -10000.0, iota8[:], ALU.mult, ALU.add),
            wait=[(vch, vch.n)], inc=((vch, 1),))
        p.emit("vector", lambda v, r=r: v.tensor_reduce(
            r[:, 17:18], r[:, 24:32], axis=AX.X, op=ALU.min),
            wait=[(vch, vch.n)], inc=((vch, 1),))
        p.emit("vector", lambda v, r=r: v.tensor_scalar(
            r[:, 18:19], r[:, 16:17], 1.0 - 1e-8, 1e-8, ALU.min, ALU.max),
            wait=[(vch, vch.n)], inc=((vch, 1),))
        p.emit("vector", lambda v, r=r: v.tensor_scalar(
            r[:, 19:20], r[:, 18:19], 0.5 + 1e-8, None, ALU.add),
            wait=[(vch, vch.n)], inc=((vch, 1),))
        p.emit("vector", lambda v, r=r: v.tensor_scalar(
            r[:, 20:21], r[:, 19:20], 0.5, 2.0, ALU.max, ALU.min),
            wait=[(vch, vch.n)], inc=((vch, 1),))
        p.emit("vector", lambda v, r=r: v.tensor_tensor(
            r[:, 24:32], iota8[:], r[:, 17:18].broadcast_to([128, 8]),
            ALU.subtract),
            wait=[(vch, vch.n)], inc=((vch, 1),))
        p.emit("vector", lambda v, r=r: v.tensor_scalar(
            r[:, 0:8], r[:, 24:32], 10000.0, None, ALU.is_equal),
            wait=[(vch, vch.n)], inc=((vch, 1),))
        p.emit("vector", lambda v, r=r: v.tensor_tensor(
            r[:, 24:32], r[:, 0:8], r[:, 18:19].broadcast_to([128, 8]),
            ALU.mult),
            wait=[(vch, vch.n)], inc=((vch, 1),))
        p.emit("vector", lambda v, r=r: v.tensor_copy(
            r[:, 31:32], r[:, 20:21]),
            wait=[(vch, vch.n)], inc=((vch, 1),))
        wexp_v = vch.n
        p.dma("sync", wexp_out[i * 128:(i + 1) * 128, :], r[:, 24:32], sto,
              wait=[(vch, wexp_v)])
        if debug:
            p.dma("sync", dbg_r[i * 128:(i + 1) * 128, :], r[:], sto,
                  wait=[(vch, wexp_v)])
            p.wait("vector", sto, sto.n)
        pe = PS[pb]
        for k in range(8):
            waits = []
            if k == 0:
                waits = [(ld, ldv), (gld, gldv)]
                if i >= 2:
                    waits.append((ev, ev.n - 1))
            for half in range(2):
                p.emit("tensor", lambda t, k=k, half=half, pb=pb, pe=pe:
                       t.matmul(pe[:, half * 512:(half + 1) * 512],
                                oTs[pb][:, k, :],
                                wos[:, k, half * 512:(half + 1) * 512],
                                start=(k == 0), stop=(k == 7)),
                       wait=waits if (k == 0 and half == 0) else [],
                       inc=((mm, 1),) if (k == 7 and half == 1) else ())
        smv = mm.n
        sw = [(mm, smv)]
        if i >= 2:
            sw.append((sto, 48 * (i - 1)))
        p.emit("scalar", lambda s, pb=pb, pe=pe: s.copy(s1s[pb][:], pe[:]),
               wait=sw, inc=((ev, 1),))
        p.dma("sync", s1_out[i * 128:(i + 1) * 128, :], s1s[pb][:], sto,
              wait=[(ev, ev.n)])
        return (sto, sto.n)

    emit_ln_tiles(p, NTILE, load_tile, out_tile, lnb,
                  dict(ld=ld, vch=vch, sch=sch))
    p.wait("sync", sto, sto.n)
    p.finalize()
    return nc



def build_l1_experts():
    nc = new_nc()
    p = Prog(nc)
    I = 2816
    NI = I // 128
    NSLAB = NT // 512

    xnT_in = nc.dram_tensor("xnT", [8, 128, NT], BF16, kind="ExternalInput").ap()
    w1t = nc.dram_tensor("w1t", [D, I], F32, kind="ExternalInput").ap()
    w3t = nc.dram_tensor("w3t", [D, I], F32, kind="ExternalInput").ap()
    w2t = nc.dram_tensor("w2t", [I, D], F32, kind="ExternalInput").ap()
    wtok = nc.dram_tensor("wtok", [128, NT // 128], F32,
                          kind="ExternalInput").ap()
    z_out = nc.dram_tensor("z", [NT, D], BF16, kind="ExternalOutput").ap()

    sb = nc.alloc_sbuf_tensor
    w1s = sb("w1s", [128, 8, I], BF16)
    w3s = sb("w3s", [128, 8, I], BF16)
    w2s = sb("w2s", [128, NI, D], BF16)
    xns = [sb(f"xns{i}", [128, 8, 512], BF16) for i in range(2)]
    h1s = sb("h1s", [128, 512], F32)
    prodT = sb("prodT", [128, NI, 512], BF16)
    wts_all = sb("wts_all", [128, NT // 128], F32)
    zs = [sb(f"zs{i}", [128, D], BF16) for i in range(2)]

    ps = nc.alloc_psum_tensor
    PH = [ps(f"PH{i}", [128, 512], F32) for i in range(2)]
    PZ = [ps(f"PZ{i}", [128, D], F32) for i in range(2)]

    ld = p.sem("ld")
    gld = p.sem("gld")
    mm = p.sem("mm")
    ev = p.sem("ev")
    sch = p.sem("sch")
    sto = p.sem("sto")

    p.dma("gpsimd", w1s[:], w1t.rearrange("(k p) i -> p k i", p=128), gld)
    p.dma("gpsimd", w3s[:], w3t.rearrange("(k p) i -> p k i", p=128), gld)
    p.dma("gpsimd", w2s[:], w2t.rearrange("(f p) c -> p f c", p=128), gld)
    p.dma("sync", wts_all[:], wtok, ld)
    p.wait("tensor", gld, gld.n)
    p.wait("vector", ld, ld.n)

    zv_vals = []
    for s in range(NSLAB):
        xs = xns[s % 2]
        w = [(mm, mm.n)] if s >= 2 else []
        p.dma("sync", xs[:], xnT_in[:, :, s * 512:(s + 1) * 512]
              .rearrange("k p t -> p k t"), ld, wait=w)
        ldv = ld.n
        for f in range(NI):
            for which in range(2):
                pb = PH[which]
                wsrc = w1s if which == 0 else w3s
                for k in range(8):
                    waits = []
                    if k == 0:
                        waits = [(ld, ldv)]
                        if s > 0 or f > 0:
                            waits.append((sch, sch.n) if which == 0
                                         else (ev, ev.n))
                    p.emit("tensor", lambda t, f=f, k=k, xs=xs, pb=pb, wsrc=wsrc:
                           t.matmul(pb[:], wsrc[:, k, f * 128:(f + 1) * 128],
                                    xs[:, k, :], start=(k == 0), stop=(k == 7)),
                           wait=waits, inc=((mm, 1),) if k == 7 else ())
                if which == 0:
                    p.emit("scalar", lambda sc, pb=pb: sc.activation(
                        h1s[:], pb[:], AF.Silu),
                        wait=[(mm, mm.n)], inc=((sch, 1),))
            p.emit("vector", lambda v, f=f: v.tensor_tensor(
                prodT[:, f, :], h1s[:], PH[1][:], ALU.mult),
                wait=[(mm, mm.n), (sch, sch.n)], inc=((ev, 1),))
        prod_done = ev.n
        for tt in range(4):
            gt = s * 4 + tt
            pz = PZ[tt % 2]
            for f in range(NI):
                waits = []
                if f == 0:
                    waits = [(ev, prod_done)]
                    if gt >= 2:
                        waits.append((ev, zv_vals[gt - 2]))
                for half in range(2):
                    p.emit("tensor", lambda t, f=f, tt=tt, half=half, pz=pz:
                           t.matmul(pz[:, half * 512:(half + 1) * 512],
                                    prodT[:, f, tt * 128:(tt + 1) * 128],
                                    w2s[:, f, half * 512:(half + 1) * 512],
                                    start=(f == 0), stop=(f == NI - 1)),
                           wait=waits if (f == 0 and half == 0) else [],
                           inc=((mm, 1),) if (f == NI - 1 and half == 1) else ())
            mv = mm.n
            zb = zs[gt % 2]
            wv = [(mm, mv)]
            if gt >= 2:
                wv.append((sto, 16 * (gt - 1)))
            p.emit("vector", lambda v, zb=zb, pz=pz, gt=gt:
                   v.tensor_scalar(zb[:], pz[:], wts_all[:, gt:gt + 1],
                                   None, ALU.mult),
                   wait=wv, inc=((ev, 1),))
            zv_vals.append(ev.n)
            p.dma("sync", z_out[gt * 128:(gt + 1) * 128, :], zb[:], sto,
                  wait=[(ev, ev.n)])
    p.wait("sync", sto, sto.n)
    p.finalize()
    return nc



def build_l1_combine():
    nc = new_nc()
    p = Prog(nc)
    TOK = TOK_PER_CORE
    NTILE = TOK // 128

    x1 = nc.dram_tensor("x1", [TOK, D], F32, kind="ExternalInput").ap()
    s1 = nc.dram_tensor("s1", [TOK, D], F32, kind="ExternalInput").ap()
    zsh = nc.dram_tensor("zsh", [TOK, D], BF16, kind="ExternalInput").ap()
    ze = nc.dram_tensor("ze", [7, TOK, D], BF16, kind="ExternalInput").ap()
    wexp = nc.dram_tensor("wexp", [TOK, 8], F32, kind="ExternalInput").ap()
    out_t = nc.dram_tensor("out", [TOK, D], F32, kind="ExternalOutput").ap()

    sb = nc.alloc_sbuf_tensor
    x1s = [sb(f"x1s{i}", [128, D], F32) for i in range(2)]
    s1s = [sb(f"s1s{i}", [128, D], F32) for i in range(2)]
    shs = [sb(f"shs{i}", [128, D], BF16) for i in range(2)]
    zes = [sb(f"zes{i}", [128, 7, D], BF16) for i in range(2)]
    wxs = [sb(f"wxs{i}", [128, 8], F32) for i in range(2)]
    acc = [sb(f"acc{i}", [128, D], F32) for i in range(2)]
    gg = [sb(f"gg{i}", [128, D], F32) for i in range(2)]
    ot = [sb(f"ot{i}", [128, D], F32) for i in range(2)]

    ld = p.sem("ld")
    vch = p.sem("vch")
    sch = p.sem("sch")
    sto = p.sem("sto")

    for i in range(NTILE):
        pb = i % 2
        sl = slice(i * 128, (i + 1) * 128)
        w = [(sto, 16 * (i - 1))] if i >= 2 else []
        p.dma("sync", x1s[pb][:], x1[sl, :], ld, wait=w)
        p.dma("sync", s1s[pb][:], s1[sl, :], ld)
        p.dma("sync", shs[pb][:], zsh[sl, :], ld)
        p.dma("sync", wxs[pb][:], wexp[sl, :], ld)
        p.dma("sync", zes[pb][:], ze[:, sl, :].rearrange("e p c -> p e c"), ld)
        ldv = ld.n
        p.emit("vector", lambda v, pb=pb: v.tensor_tensor(
            acc[pb][:], zes[pb][:, 0, :], zes[pb][:, 1, :], ALU.add),
            wait=[(ld, ldv), (vch, vch.n)], inc=((vch, 1),))
        for e in range(2, 7):
            p.emit("vector", lambda v, pb=pb, e=e: v.tensor_tensor(
                acc[pb][:], acc[pb][:], zes[pb][:, e, :], ALU.add),
                wait=[(vch, vch.n)], inc=((vch, 1),))
        p.emit("vector", lambda v, pb=pb: v.scalar_tensor_tensor(
            ot[pb][:], shs[pb][:], 0.5, acc[pb][:], ALU.mult, ALU.add),
            wait=[(vch, vch.n)], inc=((vch, 1),))
        p.emit("vector", lambda v, pb=pb: v.reciprocal(
            wxs[pb][:, 0:1], wxs[pb][:, 7:8]),
            wait=[(ld, ldv), (vch, vch.n)], inc=((vch, 1),))
        p.emit("vector", lambda v, pb=pb: v.tensor_scalar(
            acc[pb][:], ot[pb][:], wxs[pb][:, 0:1], None, ALU.mult),
            wait=[(vch, vch.n)], inc=((vch, 1),))
        p.emit("scalar", lambda s, pb=pb: s.activation(
            gg[pb][:], acc[pb][:], AF.Silu),
            wait=[(vch, vch.n)], inc=((sch, 1),))
        p.emit("vector", lambda v, pb=pb: v.tensor_tensor(
            ot[pb][:], s1s[pb][:], gg[pb][:], ALU.mult),
            wait=[(sch, sch.n)], inc=((vch, 1),))
        p.emit("vector", lambda v, pb=pb: v.tensor_tensor(
            ot[pb][:], ot[pb][:], x1s[pb][:], ALU.add),
            wait=[(vch, vch.n)], inc=((vch, 1),))
        p.dma("sync", out_t[sl, :], ot[pb][:], sto, wait=[(vch, vch.n)])
    p.wait("sync", sto, sto.n)
    p.finalize()
    return nc



def _consts_bf16():
    import ml_dtypes
    ident = np.eye(128).astype(ml_dtypes.bfloat16)
    tk = np.arange(128)[:, None]
    tq = np.arange(128)[None, :]
    triu = (tk <= tq).astype(ml_dtypes.bfloat16)
    return ident, triu


def _run(nc, in_maps):
    return run_bass_kernel_spmd(nc, in_maps, core_ids=list(range(N_CORES)))


def kernel(L0_attn_x, L0_attn_W_attn, L0_attn_W_proj, L0_ffn_router_w,
           L0_ffn_router_b, L0_ffn_w1, L0_ffn_b1, L0_ffn_w2, L0_ffn_b2,
           L0_ffn_gamma, L0_ffn_beta, L1_attn_Wq, L1_attn_Wk, L1_attn_Wv,
           L1_attn_Wf, L1_attn_g_norm_w, L1_attn_Wo, L1_ffn_router_w,
           L1_ffn_routing_bias, L1_ffn_sw1, L1_ffn_sw2, L1_ffn_sw3,
           L1_ffn_ew1, L1_ffn_ew2, L1_ffn_ew3):
    import ml_dtypes
    asc = np.ascontiguousarray
    x = np.asarray(L0_attn_x, np.float32)
    W_attn = np.asarray(L0_attn_W_attn, np.float32)
    ident, triu = _consts_bf16()
    identf = np.eye(128).astype(np.float32)
    r127 = np.zeros((128, 128), np.float32)
    r127[127, :] = 1.0

    nc = build_l0_attn()
    in_maps = []
    for c in range(N_CORES):
        b = c // 2
        hh = c % 2
        heads = range(hh * 8, hh * 8 + 8)
        wqk_a = np.stack([np.concatenate(
            [W_attn[:, h * 64:(h + 1) * 64],
             W_attn[:, 1024 + h * 64:1024 + (h + 1) * 64]], axis=1)
            for h in heads])
        wv_a = np.stack([W_attn[:, 2048 + h * 64:2048 + (h + 1) * 64]
                         for h in heads])
        in_maps.append({"xb": asc(x[b]), "wqk": asc(wqk_a), "wv": asc(wv_a),
                        "ident": ident, "triu": triu})
    res = _run(nc, in_maps)
    o0 = np.zeros((NT, D), np.float32)
    xnT_full = np.zeros((8, 128, NT), dtype=ml_dtypes.bfloat16)
    for c in range(N_CORES):
        b = c // 2
        hh = c % 2
        oT = np.asarray(res.results[c]["oT"])
        for hl in range(8):
            h = hh * 8 + hl
            rows = oT[hl // 2, (hl % 2) * 64:(hl % 2) * 64 + 64, :]
            o0[b * T:(b + 1) * T, h * 64:(h + 1) * 64] = rows.T
        if hh == 0:
            xnT_full[:, :, b * T:(b + 1) * T] = np.asarray(
                res.results[c]["xnT_o"])

    nc = build_l0_ffn()
    w1 = asc(np.asarray(L0_ffn_w1, np.float32)[0])
    b1 = asc(np.asarray(L0_ffn_b1, np.float32)[0].reshape(32, 128).T)
    w2 = asc(np.asarray(L0_ffn_w2, np.float32)[0])
    b2 = asc(np.asarray(L0_ffn_b2, np.float32)[0].reshape(1, D))
    in_maps = []
    for c in range(N_CORES):
        sl = slice(c * TOK_PER_CORE, (c + 1) * TOK_PER_CORE)
        in_maps.append({"xnT": asc(xnT_full[:, :, sl]),
                        "w1": w1, "b1": b1, "w2": w2, "b2": b2})
    res = _run(nc, in_maps)
    eo = np.concatenate([np.asarray(res.results[c]["eo"])
                         for c in range(N_CORES)], 0)

    nc = build_l0_gate()
    xf = x.reshape(NT, D)
    gamma = np.asarray(L0_ffn_gamma, np.float32).reshape(1, D)
    beta = np.asarray(L0_ffn_beta, np.float32).reshape(1, D)
    wproj = np.asarray(L0_attn_W_proj, np.float32)
    in_maps = []
    for c in range(N_CORES):
        sl = slice(c * TOK_PER_CORE, (c + 1) * TOK_PER_CORE)
        in_maps.append({"xo": asc(xf[sl]), "eo": asc(eo[sl]),
                        "oT": asc(o0[sl].T), "wproj": wproj,
                        "gamma": gamma, "beta": beta})
    res = _run(nc, in_maps)
    x1 = np.concatenate([np.asarray(res.results[c]["x1"])
                         for c in range(N_CORES)], 0)

    nc = build_l1_attn()
    Wq = np.asarray(L1_attn_Wq, np.float32)
    Wk = np.asarray(L1_attn_Wk, np.float32)
    Wv = np.asarray(L1_attn_Wv, np.float32)
    Wf = np.asarray(L1_attn_Wf, np.float32)
    gnw = np.asarray(L1_attn_g_norm_w, np.float32).reshape(1, 256)
    in_maps = []
    for c in range(N_CORES):
        b = c // 2
        hp = c % 2
        hs = [2 * hp, 2 * hp + 1]
        in_maps.append({
            "x1b": asc(x1[b * T:(b + 1) * T]),
            "wq": asc(np.stack([Wq[:, h * 256:(h + 1) * 256] for h in hs], 1)),
            "wk": asc(np.stack([Wk[:, h * 256:(h + 1) * 256] for h in hs], 1)),
            "wv": asc(np.stack([Wv[:, h * 256:(h + 1) * 256] for h in hs], 1)),
            "wf": asc(np.stack([Wf[:, h * 64:(h + 1) * 64] for h in hs], 1)),
            "gnw": gnw, "ident": ident, "identf": identf,
            "triu": triu, "r127": r127,
        })
    res = _run(nc, in_maps)
    o1 = np.zeros((NT, D), np.float32)
    for c in range(N_CORES):
        b = c // 2
        hp = c % 2
        on = np.asarray(res.results[c]["on"])
        o1[b * T:(b + 1) * T, hp * 512:(hp + 1) * 512] = on.reshape(T, 512)

    nc = build_l1_router()
    rw_pad = np.zeros((D, 8), np.float32)
    rw_pad[:, :7] = np.asarray(L1_ffn_router_w, np.float32).T
    rb_pad = np.full((1, 8), -1e9, np.float32)
    rb_pad[0, :7] = np.asarray(L1_ffn_routing_bias, np.float32)
    wo = np.asarray(L1_attn_Wo, np.float32)
    in_maps = []
    for c in range(N_CORES):
        sl = slice(c * TOK_PER_CORE, (c + 1) * TOK_PER_CORE)
        in_maps.append({"x1": asc(x1[sl]), "oT1": asc(o1[sl].T), "wo": wo,
                        "rw": rw_pad, "rb": rb_pad,
                        "ident": ident, "identf": identf})
    res = _run(nc, in_maps)
    s1 = np.concatenate([np.asarray(res.results[c]["s1"])
                         for c in range(N_CORES)], 0)
    wexp = np.concatenate([np.asarray(res.results[c]["wexp"])
                           for c in range(N_CORES)], 0)
    xn1T_full = np.concatenate([np.asarray(res.results[c]["xn1T"])
                                for c in range(N_CORES)], 2)

    nc = build_l1_experts()
    ew1 = np.asarray(L1_ffn_ew1, np.float32)
    ew2 = np.asarray(L1_ffn_ew2, np.float32)
    ew3 = np.asarray(L1_ffn_ew3, np.float32)
    sw1 = np.asarray(L1_ffn_sw1, np.float32)
    sw2 = np.asarray(L1_ffn_sw2, np.float32)
    sw3 = np.asarray(L1_ffn_sw3, np.float32)
    in_maps = []
    for c in range(N_CORES):
        if c < 7:
            w1e, w2e, w3e = ew1[c], ew2[c], ew3[c]
            wt = asc(wexp[:, c].reshape(-1, 128).T)
        else:
            w1e, w2e, w3e = sw1, sw2, sw3
            wt = asc(np.ones((NT // 128, 128), np.float32).T)
        in_maps.append({"xnT": xn1T_full, "w1t": asc(w1e.T),
                        "w3t": asc(w3e.T), "w2t": asc(w2e.T), "wtok": wt})
    res = _run(nc, in_maps)
    z = [np.asarray(res.results[c]["z"]) for c in range(N_CORES)]

    nc = build_l1_combine()
    in_maps = []
    for c in range(N_CORES):
        sl = slice(c * TOK_PER_CORE, (c + 1) * TOK_PER_CORE)
        in_maps.append({
            "x1": asc(x1[sl]), "s1": asc(s1[sl]), "zsh": asc(z[7][sl]),
            "ze": asc(np.stack([z[e][sl] for e in range(7)])),
            "wexp": asc(wexp[sl]),
        })
    res = _run(nc, in_maps)
    out = np.concatenate([np.asarray(res.results[c]["out"])
                          for c in range(N_CORES)], 0)
    return out.reshape(B, T, D)


# version 02 (final)
# speedup vs baseline: 1.0256x; 1.0256x over previous
import numpy as np
import concourse.bass as bass
import concourse.mybir as mybir
from concourse.bass_utils import run_bass_kernel_spmd

F32 = mybir.dt.float32
BF16 = mybir.dt.bfloat16
AF = mybir.ActivationFunctionType
ALU = mybir.AluOpType
AX = mybir.AxisListType

N_CORES = 8
B, T, D = 4, 4096, 1024
NT = B * T
TOK_PER_CORE = NT // N_CORES



class Sem:
    __slots__ = ("h", "n")

    def __init__(self, h):
        self.h = h
        self.n = 0


class Prog:

    ENGINES = ("tensor", "vector", "scalar", "gpsimd", "sync")

    def __init__(self, nc):
        self.nc = nc
        self.q = {e: [] for e in self.ENGINES}
        self._semid = 0

    def sem(self, name=None):
        h = self.nc.alloc_semaphore(name or f"s{self._semid}")
        self._semid += 1
        return Sem(h)

    def emit(self, eng, fn, wait=(), inc=()):
        wait = tuple((s, v) for s, v in wait)
        inc = tuple(inc)
        for s, a in inc:
            s.n += a

        def run(e, fn=fn, wait=wait, inc=inc):
            for s, v in wait:
                e.wait_ge(s.h, v)
            inst = fn(e)
            for s, a in inc:
                inst = inst.then_inc(s.h, a)

        self.q[eng].append(run)

    def wait(self, eng, sem, val):
        self.q[eng].append(lambda e, s=sem, v=val: e.wait_ge(s.h, v))

    def dma(self, eng, out, in_, sem, wait=()):
        return self.emit(eng, lambda e: e.dma_start(out=out, in_=in_),
                         wait=wait, inc=((sem, 16),))

    def finalize(self):
        with self.nc.Block() as block:
            for name in self.ENGINES:
                ops = self.q[name]
                if not ops:
                    continue
                getattr(block, name)(lambda eng, ops=ops: [op(eng) for op in ops])


def new_nc():
    return bass.Bass("TRN2", target_bir_lowering=False,
                     detect_race_conditions=False)


def setup_consts(p, values, gq):
    nc = p.nc
    for v in values:
        t = nc.alloc_sbuf_tensor(
            f"cst_{str(v).replace('.', '_').replace('-', 'm')}", [128, 1], F32)
        nc.const_aps.register(F32, float(v), t.ap())
        p.emit("gpsimd", lambda g, t=t, v=v: g.memset(t[:], v), inc=((gq, 1),))
    return gq.n



def emit_ln_tiles(p, n_tiles, load_tile, out_tile, ln, sems):
    Dd = ln["D"]
    ld, vch, sch = sems["ld"], sems["vch"], sems["sch"]
    done_tok = []
    for i in range(n_tiles):
        pb = i % 2
        x, xsq, st, xn = ln["x"][pb], ln["xsq"][pb], ln["st"][pb], ln["xn"][pb]
        waits = []
        if i >= 2:
            waits.append(done_tok[i - 2])
        load_tile(i, x, waits)
        ldv = ld.n
        p.emit("vector", lambda v, x=x, st=st:
               v.reduce_sum(st[:, 0:1], x[:], axis=AX.X),
               wait=[(ld, ldv)], inc=((vch, 1),))
        p.emit("scalar", lambda s, x=x, xsq=xsq, st=st:
               s.activation(xsq[:], x[:], AF.Square, accum_out=st[:, 1:2]),
               wait=[(ld, ldv)], inc=((sch, 1),))
        p.emit("vector", lambda v, st=st: v.tensor_scalar(
            st[:, 2:3], st[:, 0:1], 1.0 / Dd, None, ALU.mult),
            wait=[(vch, vch.n)], inc=((vch, 1),))
        p.emit("vector", lambda v, st=st: v.tensor_scalar(
            st[:, 3:4], st[:, 1:2], 1.0 / Dd, None, ALU.mult),
            wait=[(sch, sch.n)], inc=((vch, 1),))
        p.emit("vector", lambda v, st=st: v.tensor_tensor(
            st[:, 4:5], st[:, 2:3], st[:, 2:3], ALU.mult),
            wait=[(vch, vch.n)], inc=((vch, 1),))
        p.emit("vector", lambda v, st=st: v.tensor_tensor(
            st[:, 5:6], st[:, 3:4], st[:, 4:5], ALU.subtract),
            wait=[(vch, vch.n)], inc=((vch, 1),))
        p.emit("scalar", lambda s, st=st: s.activation(
            st[:, 6:7], st[:, 5:6], AF.Sqrt, bias=1e-5),
            wait=[(vch, vch.n)], inc=((sch, 1),))
        p.emit("vector", lambda v, st=st: v.reciprocal(st[:, 7:8], st[:, 6:7]),
               wait=[(sch, sch.n)], inc=((vch, 1),))
        p.emit("vector", lambda v, x=x, xn=xn, st=st: v.tensor_scalar(
            xn[:], x[:], st[:, 2:3], st[:, 7:8], ALU.subtract, ALU.mult),
            wait=[(vch, vch.n)], inc=((vch, 1),))
        done_tok.append(out_tile(i, xn, (vch, vch.n)))
    return done_tok



def build_l0_attn():
    nc = new_nc()
    p = Prog(nc)
    NTILE = T // 128

    xb = nc.dram_tensor("xb", [T, D], F32, kind="ExternalInput").ap()
    wqk = nc.dram_tensor("wqk", [8, D, 128], F32, kind="ExternalInput").ap()
    wv = nc.dram_tensor("wv", [8, D, 64], F32, kind="ExternalInput").ap()
    ident = nc.dram_tensor("ident", [128, 128], BF16, kind="ExternalInput").ap()
    triu = nc.dram_tensor("triu", [128, 128], BF16, kind="ExternalInput").ap()
    oT_out = nc.dram_tensor("oT", [4, 128, T], F32, kind="ExternalOutput").ap()
    xnT_out = nc.dram_tensor("xnT_o", [8, 128, T], BF16,
                             kind="ExternalOutput").ap()

    sb = nc.alloc_sbuf_tensor
    xnT = sb("xnT", [128, 8, T], BF16)
    qT = sb("qT", [64, T], BF16)
    kT = sb("kT", [64, T], BF16)
    vT = sb("vT", [64, T], BF16)
    vt = sb("vt", [128, NTILE, 65], BF16)
    ptb = [sb(f"ptb{i}", [128, 1024], BF16) for i in range(2)]
    osb = sb("osb", [128, 4, T], F32)
    lrow = sb("lrow", [1, 1024], F32)
    lrow2 = sb("lrow2", [1, 1024], F32)
    lrowb = sb("lrowb", [1, 1024], BF16)
    otmp = sb("otmp", [64, 1024], F32)
    ident_s = sb("ident_s", [128, 128], BF16)
    triu_s = sb("triu_s", [128, 128], BF16)
    ones1x64 = sb("ones1x64", [1, 64], BF16)
    wqk_s = [sb(f"wqk_s{i}", [128, 8, 128], BF16) for i in range(2)]
    wv_s = [sb(f"wv_s{i}", [128, 8, 64], BF16) for i in range(2)]
    lnb = {
        "D": D,
        "x": [sb(f"lnx{i}", [128, D], F32) for i in range(2)],
        "xsq": [sb(f"lnq{i}", [128, D], F32) for i in range(2)],
        "st": [sb(f"lns{i}", [128, 8], F32) for i in range(2)],
        "xn": [sb(f"lnn{i}", [128, D], BF16) for i in range(2)],
    }

    ps = nc.alloc_psum_tensor
    P1 = ps("P1", [128, 1024], F32)
    P2 = ps("P2", [128, 1024], F32)
    P3 = ps("P3", [128, 2048], BF16)
    P4 = ps("P4", [64, 1024], F32)

    ld = p.sem("ld")
    vch = p.sem("vch")
    sch = p.sem("sch")
    mm = p.sem("mm")
    tr = p.sem("tr")
    wld = p.sem("wld")
    gmc = p.sem("gmc")
    st = p.sem("st")

    cv = setup_consts(p, [-4.0, 0.125, 1e-5], gmc)
    p.wait("scalar", gmc, cv)
    p.dma("sync", ident_s[:], ident, ld)
    p.dma("sync", triu_s[:], triu, ld)
    p.emit("gpsimd", lambda g: g.memset(ones1x64[:], 1.0), inc=((gmc, 1),))
    p.emit("gpsimd", lambda g: g.memset(vt[:], 1.0), inc=((gmc, 1),))
    consts_ld = ld.n
    consts_g = gmc.n
    p.wait("tensor", gmc, consts_g)

    def load_tile(i, xbuf, waits):
        p.dma("sync", xbuf[:], xb[i * 128:(i + 1) * 128, :], ld, wait=waits)

    def out_tile(i, xn_ap, ready):
        for k in range(8):
            half = k % 2
            w = [ready, (ld, consts_ld)]
            if i > 0 or k >= 2:
                w.append((tr, tr.n - 1))
            p.emit("tensor", lambda t, xn_ap=xn_ap, k=k, half=half:
                   t.transpose(P3[:, half * 1024:half * 1024 + 128],
                               xn_ap[:, k * 128:(k + 1) * 128], ident_s[:]),
                   wait=w, inc=((mm, 1),))
            mv = mm.n
            p.emit("scalar", lambda s, i=i, k=k, half=half:
                   s.copy(xnT[:, k, i * 128:(i + 1) * 128],
                          P3[:, half * 1024:half * 1024 + 128]),
                   wait=[(mm, mv)], inc=((tr, 1),))
        return (tr, tr.n)

    emit_ln_tiles(p, NTILE, load_tile, out_tile, lnb,
                  dict(ld=ld, vch=vch, sch=sch))
    xnT_ready = (tr, tr.n)

    p.dma("gpsimd", wqk_s[0][:], wqk[0].rearrange("(k p) c -> p k c", p=128), wld)
    p.dma("gpsimd", wv_s[0][:], wv[0].rearrange("(k p) c -> p k c", p=128), wld)

    NQG = 4
    for h in range(8):
        hp = h % 2
        wq = wqk_s[hp]
        wvs = wv_s[hp]
        wldv = wld.n

        for ts in range(8):
            waits = [(wld, wldv), xnT_ready]
            if ts > 0:
                waits.append((tr, tr.n))
            for k in range(8):
                p.emit("tensor", lambda t, k=k, ts=ts, wq=wq:
                       t.matmul(P1[:, 0:512], wq[:, k, :],
                                xnT[:, k, ts * 512:(ts + 1) * 512],
                                start=(k == 0), stop=(k == 7)),
                       wait=waits if k == 0 else [],
                       inc=((mm, 1),) if k == 7 else ())
            mv = mm.n
            p.emit("scalar", lambda s, ts=ts:
                   s.copy(qT[:, ts * 512:(ts + 1) * 512], P1[0:64, 0:512]),
                   wait=[(mm, mv)], inc=((sch, 1),))
            p.emit("vector", lambda v, ts=ts:
                   v.tensor_copy(kT[:, ts * 512:(ts + 1) * 512], P1[64:128, 0:512]),
                   wait=[(mm, mv)], inc=((tr, 1),))
            p.wait("tensor", sch, sch.n)
        qk_done = (tr.n, sch.n)

        for ts in range(8):
            waits = [(tr, tr.n)]
            for k in range(8):
                p.emit("tensor", lambda t, k=k, ts=ts, wvs=wvs:
                       t.matmul(P1[0:64, 0:512], wvs[:, k, :],
                                xnT[:, k, ts * 512:(ts + 1) * 512],
                                start=(k == 0), stop=(k == 7)),
                       wait=waits if k == 0 else [],
                       inc=((mm, 1),) if k == 7 else ())
            mv = mm.n
            p.emit("vector", lambda v, ts=ts:
                   v.tensor_copy(vT[:, ts * 512:(ts + 1) * 512], P1[0:64, 0:512]),
                   wait=[(mm, mv)], inc=((tr, 1),))
        v_done_tr = tr.n

        if h < 7:
            p.dma("gpsimd", wqk_s[(h + 1) % 2][:],
                  wqk[h + 1].rearrange("(k p) c -> p k c", p=128), wld,
                  wait=[(mm, mm.n)])
            p.dma("gpsimd", wv_s[(h + 1) % 2][:],
                  wv[h + 1].rearrange("(k p) c -> p k c", p=128), wld,
                  wait=[(mm, mm.n)])

        for j in range(NTILE):
            half = j % 2
            w = [(tr, v_done_tr)] if j < 2 else [(tr, tr.n - 1)]
            p.emit("tensor", lambda t, j=j, half=half:
                   t.transpose(P3[:, half * 1024:half * 1024 + 64],
                               vT[:, j * 128:(j + 1) * 128],
                               ident_s[0:64, 0:64]),
                   wait=w, inc=((mm, 1),))
            p.emit("scalar", lambda s, j=j, half=half:
                   s.copy(vt[:, j, 0:64], P3[:, half * 1024:half * 1024 + 64]),
                   wait=[(mm, mm.n)], inc=((tr, 1),))
        vt_done = tr.n

        pO = P2[0:65, :]
        for g in range(NQG):
            ilo, ihi = 8 * g, 8 * g + 8
            for j in range(ihi):
                i_start = max(j, ilo)
                nq = (ihi - i_start) * 128
                tq0 = i_start * 128
                ptile = ptb[j % 2]
                nchunks = (nq + 511) // 512
                for c in range(nchunks):
                    cw = min(512, nq - c * 512)
                    waits = []
                    if c == 0:
                        if g == 0 and j == 0:
                            waits.append((tr, vt_done))
                        waits.append((sch, sch.n))
                    p.emit("tensor", lambda t, c=c, cw=cw, tq0=tq0, j=j:
                           t.matmul(P1[:, c * 512:c * 512 + cw],
                                    kT[:, j * 128:(j + 1) * 128],
                                    qT[:, tq0 + c * 512:tq0 + c * 512 + cw],
                                    start=True, stop=True),
                           wait=waits, inc=((mm, 1),))
                mmv = mm.n
                p.emit("scalar", lambda s, ptile=ptile, nq=nq:
                       s.activation(ptile[:, 0:nq], P1[:, 0:nq], AF.Exp,
                                    bias=-4.0, scale=0.125),
                       wait=[(mm, mmv)], inc=((sch, 1),))
                expv = sch.n
                pv_wait = [(sch, expv)]
                if j >= ilo:
                    p.emit("vector", lambda v, ptile=ptile:
                           v.tensor_tensor(ptile[:, 0:128], ptile[:, 0:128],
                                           triu_s[:], ALU.mult),
                           wait=[(sch, expv), (ld, consts_ld)], inc=((vch, 1),))
                    pv_wait = [(vch, vch.n)]
                if j == 0:
                    for c in range(2):
                        p.emit("tensor", lambda t, c=c, ptile=ptile:
                               t.matmul(pO[:, c * 512:(c + 1) * 512], vt[:, 0, :],
                                        ptile[:, c * 512:(c + 1) * 512],
                                        start=True, stop=False),
                               wait=pv_wait if c == 0 else [],
                               inc=((mm, 1),) if c == 1 else ())
                else:
                    n_i = nq // 128
                    for ii in range(n_i):
                        iblk = i_start + ii
                        col0 = (iblk - ilo) * 128
                        p.emit("tensor", lambda t, ii=ii, col0=col0, j=j,
                               iblk=iblk, ptile=ptile:
                               t.matmul(pO[:, col0:col0 + 128], vt[:, j, :],
                                        ptile[:, ii * 128:(ii + 1) * 128],
                                        start=False, stop=(j == iblk)),
                               wait=pv_wait if ii == 0 else [],
                               inc=((mm, 1),) if ii == n_i - 1 else ())
            grp_done = mm.n
            p.emit("vector", lambda v: v.tensor_copy(lrow[:], pO[64:65, :]),
                   wait=[(mm, grp_done)], inc=((vch, 1),))
            p.emit("vector", lambda v: v.reciprocal(lrow2[:], lrow[:]),
                   wait=[(vch, vch.n)], inc=((vch, 1),))
            p.emit("vector", lambda v: v.tensor_copy(lrowb[:], lrow2[:]),
                   wait=[(vch, vch.n)], inc=((vch, 1),))
            linv_v = vch.n
            p.emit("tensor", lambda t: t.matmul(
                P4[:, 0:512], ones1x64[:], lrowb[:, 0:512], start=True, stop=True),
                wait=[(vch, linv_v), (ld, consts_ld)])
            p.emit("tensor", lambda t: t.matmul(
                P4[:, 512:1024], ones1x64[:], lrowb[:, 512:1024],
                start=True, stop=True), inc=((mm, 1),))
            bc_v = mm.n
            p.emit("scalar", lambda s: s.copy(otmp[:], pO[0:64, :]),
                   wait=[(mm, bc_v)], inc=((sch, 1),))
            p.emit("vector", lambda v, h=h, g=g: v.tensor_tensor(
                osb[(h % 2) * 64:(h % 2) * 64 + 64, h // 2,
                    g * 1024:(g + 1) * 1024],
                otmp[:], P4[:, :], ALU.mult),
                wait=[(sch, sch.n)], inc=((vch, 1),))
            p.wait("tensor", sch, sch.n)
            p.wait("tensor", vch, vch.n)
    osb_done = vch.n

    p.wait("sync", vch, osb_done)
    p.wait("sync", sch, sch.n)
    for c in range(4):
        p.dma("sync", oT_out[c], osb[:, c, :], st)
    p.wait("sync", st, st.n)

    p.finalize()
    return nc



def build_l0_ffn():
    nc = new_nc()
    p = Prog(nc)
    TOK = TOK_PER_CORE
    NSLAB = TOK // 512
    F = 4096
    NF = F // 128

    xnT_in = nc.dram_tensor("xnT", [8, 128, TOK], BF16, kind="ExternalInput").ap()
    w1 = nc.dram_tensor("w1", [D, F], F32, kind="ExternalInput").ap()
    b1 = nc.dram_tensor("b1", [128, F // 128], F32, kind="ExternalInput").ap()
    w2 = nc.dram_tensor("w2", [F, D], F32, kind="ExternalInput").ap()
    b2 = nc.dram_tensor("b2", [1, D], F32, kind="ExternalInput").ap()
    eo_out = nc.dram_tensor("eo", [TOK, D], F32, kind="ExternalOutput").ap()

    sb = nc.alloc_sbuf_tensor
    w1s = sb("w1s", [128, 8, F], BF16)
    w2s = sb("w2s", [128, NF, D], BF16)
    b1s = sb("b1s", [128, NF], F32)
    b2s = sb("b2s", [1, D], BF16)
    ones1 = sb("ones1", [1, 128], BF16)
    xns = [sb(f"xns{i}", [128, 8, 512], BF16) for i in range(2)]
    h1T = sb("h1T", [128, NF, 512], BF16)
    eos = [sb(f"eos{i}", [128, D], F32) for i in range(2)]

    ps = nc.alloc_psum_tensor
    PH = [ps(f"PH{i}", [128, 512], F32) for i in range(2)]
    PE_ = [ps(f"PE{i}", [128, D], F32) for i in range(2)]

    ld = p.sem("ld")
    gld = p.sem("gld")
    gmc = p.sem("gmc")
    mm = p.sem("mm")
    ev = p.sem("ev")
    sto = p.sem("sto")

    p.dma("gpsimd", w1s[:], w1.rearrange("(k p) f -> p k f", p=128), gld)
    p.dma("gpsimd", w2s[:], w2.rearrange("(f p) c -> p f c", p=128), gld)
    p.dma("sync", b1s[:], b1, ld)
    p.dma("gpsimd", b2s[:], b2, gld)
    p.emit("gpsimd", lambda g: g.memset(ones1[:], 1.0), inc=((gmc, 1),))
    wld = ld.n
    p.wait("tensor", gld, gld.n)
    p.wait("tensor", gmc, gmc.n)
    p.wait("vector", gld, gld.n)
    eo_evac_vals = []

    for s in range(NSLAB):
        xs = xns[s % 2]
        w = [(mm, mm.n)] if s >= 2 else []
        p.dma("sync", xs[:], xnT_in[:, :, s * 512:(s + 1) * 512]
              .rearrange("k p t -> p k t"), ld, wait=w)
        ldv = ld.n
        for f in range(NF):
            pb = PH[f % 2]
            for k in range(8):
                waits = []
                if k == 0:
                    waits = [(ld, ldv)]
                    if s > 0 or f >= 2:
                        waits.append((ev, ev.n - 1))
                p.emit("tensor", lambda t, f=f, k=k, xs=xs, pb=pb:
                       t.matmul(pb[:], w1s[:, k, f * 128:(f + 1) * 128],
                                xs[:, k, :], start=(k == 0), stop=(k == 7)),
                       wait=waits, inc=((mm, 1),) if k == 7 else ())
            mv = mm.n
            p.emit("vector", lambda v, f=f, pb=pb: v.tensor_scalar(
                h1T[:, f, :], pb[:], b1s[:, f:f + 1], 0.0, ALU.add, ALU.max),
                wait=[(mm, mv)], inc=((ev, 1),))
        h1_done = ev.n
        for tt in range(4):
            pe = PE_[tt % 2]
            gt = s * 4 + tt
            for f in range(NF):
                waits = []
                if f == 0:
                    waits = [(ev, h1_done)]
                    if gt >= 2:
                        waits.append((ev, eo_evac_vals[gt - 2]))
                for half in range(2):
                    p.emit("tensor", lambda t, f=f, tt=tt, half=half, pe=pe:
                           t.matmul(pe[:, half * 512:(half + 1) * 512],
                                    h1T[:, f, tt * 128:(tt + 1) * 128],
                                    w2s[:, f, half * 512:(half + 1) * 512],
                                    start=(f == 0), stop=False),
                           wait=waits if (f == 0 and half == 0) else [])
            for half in range(2):
                p.emit("tensor", lambda t, half=half, pe=pe:
                       t.matmul(pe[:, half * 512:(half + 1) * 512], ones1[:],
                                b2s[:, half * 512:(half + 1) * 512],
                                start=False, stop=True),
                       inc=((mm, 1),) if half == 1 else ())
            mv = mm.n
            eb = eos[gt % 2]
            wv = [(mm, mv)]
            if gt >= 2:
                wv.append((sto, 16 * (gt - 1)))
            p.emit("scalar", lambda sc, eb=eb, pe=pe: sc.copy(eb[:], pe[:]),
                   wait=wv, inc=((ev, 1),))
            eo_evac_vals.append(ev.n)
            p.dma("sync", eo_out[gt * 128:(gt + 1) * 128, :], eb[:], sto,
                  wait=[(ev, ev.n)])
    p.wait("sync", sto, sto.n)
    p.finalize()
    return nc



def build_l0_gate():
    nc = new_nc()
    p = Prog(nc)
    TOK = TOK_PER_CORE
    NTILE = TOK // 128

    xo = nc.dram_tensor("xo", [TOK, D], F32, kind="ExternalInput").ap()
    eo = nc.dram_tensor("eo", [TOK, D], F32, kind="ExternalInput").ap()
    oT = nc.dram_tensor("oT", [D, TOK], F32, kind="ExternalInput").ap()
    wproj = nc.dram_tensor("wproj", [D, D], F32, kind="ExternalInput").ap()
    gamma = nc.dram_tensor("gamma", [1, D], F32, kind="ExternalInput").ap()
    beta = nc.dram_tensor("beta", [1, D], F32, kind="ExternalInput").ap()
    x1_out = nc.dram_tensor("x1", [TOK, D], F32, kind="ExternalOutput").ap()

    sb = nc.alloc_sbuf_tensor
    wps = sb("wps", [128, 8, D], BF16)
    gam = sb("gam", [1, D], F32)
    bet = sb("bet", [1, D], F32)
    ones1f = sb("ones1f", [1, 128], F32)
    gamb = sb("gamb", [128, D], F32)
    betb = sb("betb", [128, D], F32)
    oTs = [sb(f"oTs{i}", [128, 8, 128], BF16) for i in range(2)]
    eos = [sb(f"eos{i}", [128, D], F32) for i in range(2)]
    xx = [sb(f"xx{i}", [128, D], F32) for i in range(2)]
    ll = [sb(f"ll{i}", [128, D], F32) for i in range(2)]
    gg = [sb(f"gg{i}", [128, D], F32) for i in range(2)]
    x1s = [sb(f"x1s{i}", [128, D], F32) for i in range(2)]
    lnb = {
        "D": D,
        "x": [sb(f"lnx{i}", [128, D], F32) for i in range(2)],
        "xsq": [sb(f"lnq{i}", [128, D], F32) for i in range(2)],
        "st": [sb(f"lns{i}", [128, 8], F32) for i in range(2)],
        "xn": [sb(f"lnn{i}", [128, D], F32) for i in range(2)],
    }
    st2 = [sb(f"st2_{i}", [128, 8], F32) for i in range(2)]

    ps = nc.alloc_psum_tensor
    PS = [ps(f"PS{i}", [128, D], F32) for i in range(2)]
    PB = ps("PB", [128, D], F32)

    ld = p.sem("ld")
    gld = p.sem("gld")
    vch = p.sem("vch")
    sch = p.sem("sch")
    mm = p.sem("mm")
    ev = p.sem("ev")
    sto = p.sem("sto")
    gmc = p.sem("gmc")

    cv = setup_consts(p, [1e-5], gmc)
    p.wait("scalar", gmc, cv)
    p.dma("gpsimd", wps[:], wproj.rearrange("(k p) c -> p k c", p=128), gld)
    p.dma("sync", gam[:], gamma, ld)
    p.dma("sync", bet[:], beta, ld)
    p.emit("gpsimd", lambda g: g.memset(ones1f[:], 1.0), inc=((gmc, 1),))
    wld = ld.n
    onesv = gmc.n
    p.wait("tensor", gmc, onesv)
    p.wait("tensor", gld, 16)
    for half in range(2):
        p.emit("tensor", lambda t, half=half: t.matmul(
            PB[:, half * 512:(half + 1) * 512], ones1f[:],
            gam[:, half * 512:(half + 1) * 512], start=True, stop=True),
            wait=[(ld, wld)] if half == 0 else [], inc=((mm, 1),))
    p.emit("vector", lambda v: v.tensor_copy(gamb[:], PB[:]),
           wait=[(mm, mm.n)], inc=((vch, 1),))
    p.wait("tensor", vch, vch.n)
    for half in range(2):
        p.emit("tensor", lambda t, half=half: t.matmul(
            PB[:, half * 512:(half + 1) * 512], ones1f[:],
            bet[:, half * 512:(half + 1) * 512], start=True, stop=True),
            inc=((mm, 1),))
    p.emit("vector", lambda v: v.tensor_copy(betb[:], PB[:]),
           wait=[(mm, mm.n)], inc=((vch, 1),))
    gb_ready = (vch, vch.n)

    def load_tile(i, xbuf, waits):
        p.dma("sync", xbuf[:], xo[i * 128:(i + 1) * 128, :], ld, wait=waits)
        p.dma("sync", eos[i % 2][:], eo[i * 128:(i + 1) * 128, :], ld)
        ow = [(mm, mm.n)] if i >= 2 else []
        p.dma("gpsimd", oTs[i % 2][:],
              oT[:, i * 128:(i + 1) * 128].rearrange("(k p) t -> p k t", p=128),
              gld, wait=ow)

    def out_tile(i, xn_ap, ready):
        pb = i % 2
        ldv = ld.n
        gldv = gld.n
        pe = PS[pb]
        for k in range(8):
            waits = []
            if k == 0:
                waits = [(ld, ldv), (gld, gldv), (vch, vch.n)]
                if i >= 2:
                    waits.append((ev, ev.n - 1))
            for half in range(2):
                p.emit("tensor", lambda t, k=k, half=half, pb=pb, pe=pe:
                       t.matmul(pe[:, half * 512:(half + 1) * 512],
                                oTs[pb][:, k, :],
                                wps[:, k, half * 512:(half + 1) * 512],
                                start=(k == 0), stop=(k == 7)),
                       wait=waits if (k == 0 and half == 0) else [],
                       inc=((mm, 1),) if (k == 7 and half == 1) else ())
        smv = mm.n
        p.emit("vector", lambda v, pb=pb, xn_ap=xn_ap:
               v.tensor_tensor(xx[pb][:], xn_ap[:], eos[pb][:], ALU.add),
               wait=[ready], inc=((vch, 1),))
        p.emit("vector", lambda v, pb=pb: v.reduce_sum(
            st2[pb][:, 0:1], xx[pb][:], axis=AX.X),
            wait=[(vch, vch.n)], inc=((vch, 1),))
        p.emit("scalar", lambda s, pb=pb: s.activation(
            lnb["xsq"][pb][:], xx[pb][:], AF.Square, accum_out=st2[pb][:, 1:2]),
            wait=[(vch, vch.n - 1)], inc=((sch, 1),))
        p.emit("vector", lambda v, pb=pb: v.tensor_scalar(
            st2[pb][:, 2:3], st2[pb][:, 0:1], 1.0 / D, None, ALU.mult),
            wait=[(vch, vch.n)], inc=((vch, 1),))
        p.emit("vector", lambda v, pb=pb: v.tensor_scalar(
            st2[pb][:, 3:4], st2[pb][:, 1:2], 1.0 / D, None, ALU.mult),
            wait=[(sch, sch.n)], inc=((vch, 1),))
        p.emit("vector", lambda v, pb=pb: v.tensor_tensor(
            st2[pb][:, 4:5], st2[pb][:, 2:3], st2[pb][:, 2:3], ALU.mult),
            wait=[(vch, vch.n)], inc=((vch, 1),))
        p.emit("vector", lambda v, pb=pb: v.tensor_tensor(
            st2[pb][:, 5:6], st2[pb][:, 3:4], st2[pb][:, 4:5], ALU.subtract),
            wait=[(vch, vch.n)], inc=((vch, 1),))
        p.emit("scalar", lambda s, pb=pb: s.activation(
            st2[pb][:, 6:7], st2[pb][:, 5:6], AF.Sqrt, bias=1e-5),
            wait=[(vch, vch.n)], inc=((sch, 1),))
        p.emit("vector", lambda v, pb=pb: v.reciprocal(
            st2[pb][:, 7:8], st2[pb][:, 6:7]),
            wait=[(sch, sch.n)], inc=((vch, 1),))
        p.emit("vector", lambda v, pb=pb: v.tensor_scalar(
            ll[pb][:], xx[pb][:], st2[pb][:, 2:3], st2[pb][:, 7:8],
            ALU.subtract, ALU.mult),
            wait=[(vch, vch.n)], inc=((vch, 1),))
        p.emit("vector", lambda v, pb=pb: v.tensor_tensor(
            ll[pb][:], ll[pb][:], gamb[:], ALU.mult),
            wait=[(vch, vch.n), gb_ready], inc=((vch, 1),))
        p.emit("vector", lambda v, pb=pb: v.tensor_tensor(
            ll[pb][:], ll[pb][:], betb[:], ALU.add),
            wait=[(vch, vch.n)], inc=((vch, 1),))
        p.emit("scalar", lambda s, pb=pb: s.activation(
            gg[pb][:], ll[pb][:], AF.Silu),
            wait=[(vch, vch.n)], inc=((sch, 1),))
        gv = sch.n
        xw = [(sch, gv), (mm, smv)]
        if i >= 2:
            xw.append((sto, 16 * (i - 1)))
        p.emit("vector", lambda v, pb=pb, pe=pe: v.tensor_tensor(
            x1s[pb][:], pe[:], gg[pb][:], ALU.mult),
            wait=xw, inc=((vch, 1),))
        p.emit("vector", lambda v, pb=pb, xbuf=lnb["x"][pb]: v.tensor_tensor(
            x1s[pb][:], x1s[pb][:], xbuf[:], ALU.add),
            wait=[(vch, vch.n)], inc=((ev, 1),))
        p.dma("sync", x1_out[i * 128:(i + 1) * 128, :], x1s[pb][:], sto,
              wait=[(ev, ev.n)])
        return (sto, sto.n)

    emit_ln_tiles(p, NTILE, load_tile, out_tile, lnb,
                  dict(ld=ld, vch=vch, sch=sch))
    p.wait("sync", sto, sto.n)
    p.finalize()
    return nc



def build_l1_attn(debug=False):
    nc = new_nc()
    p = Prog(nc)
    NCH = T // 128
    NSL = T // 512

    x1b = nc.dram_tensor("x1b", [T, D], F32, kind="ExternalInput").ap()
    wq = nc.dram_tensor("wq", [D, 2, 256], F32, kind="ExternalInput").ap()
    wk = nc.dram_tensor("wk", [D, 2, 256], F32, kind="ExternalInput").ap()
    wv = nc.dram_tensor("wv", [D, 2, 256], F32, kind="ExternalInput").ap()
    wf = nc.dram_tensor("wf", [D, 2, 64], F32, kind="ExternalInput").ap()
    gnw = nc.dram_tensor("gnw", [1, 256], F32, kind="ExternalInput").ap()
    ident = nc.dram_tensor("ident", [128, 128], BF16, kind="ExternalInput").ap()
    identf = nc.dram_tensor("identf", [128, 128], F32, kind="ExternalInput").ap()
    triu = nc.dram_tensor("triu", [128, 128], BF16, kind="ExternalInput").ap()
    r127 = nc.dram_tensor("r127", [128, 128], F32, kind="ExternalInput").ap()
    on_out = nc.dram_tensor("on", [T, 2, 256], F32, kind="ExternalOutput").ap()
    if debug:
        dbg_stbf = nc.dram_tensor("dbg_stbf", [64, T], F32, kind="ExternalOutput").ap()
        dbg_stok = nc.dram_tensor("dbg_stok", [128, NCH, 64], F32, kind="ExternalOutput").ap()
        dbg_ttok = nc.dram_tensor("dbg_ttok", [128, NCH, 64], F32, kind="ExternalOutput").ap()
        dbg_ab = nc.dram_tensor("dbg_ab", [4, 128, 128], F32, kind="ExternalOutput").ap()
        dbg_lg = nc.dram_tensor("dbg_lg", [4, 128, 64], F32, kind="ExternalOutput").ap()
        dbg_eh = nc.dram_tensor("dbg_eh", [4, 128, 64], F32, kind="ExternalOutput").ap()
        dbg_wb = nc.dram_tensor("dbg_wb", [4, 128, 128], F32, kind="ExternalOutput").ap()
        dbg_etb = nc.dram_tensor("dbg_etb", [4, 64, 128], F32, kind="ExternalOutput").ap()
        dbg_pw = nc.dram_tensor("dbg_pw", [4, 128, 128], F32, kind="ExternalOutput").ap()
        pass

    sb = nc.alloc_sbuf_tensor
    xnT = sb("xnT", [128, 8, T], BF16)
    QT = sb("QT", [128, 2, T], BF16)
    KT = sb("KT", [128, 2, T], BF16)
    VTs = sb("VTs", [128, 2, 512], BF16)
    Ktok = sb("Ktok", [128, NCH, 256], BF16)
    Vtok = sb("Vtok", [128, NCH, 256], BF16)
    gpk = sb("gpk", [64, 2048], F32)
    ga = gpk[:, 0:512]
    gb = gpk[:, 512:1024]
    gc = gpk[:, 1024:1536]
    gd = gpk[:, 1536:2048]
    tLc = sb("tLc", [64, NCH], F32)
    STbf = sb("STbf", [64, T], BF16)
    Stok = sb("Stok", [128, NCH, 64], BF16)
    Ttok = sb("Ttok", [128, NCH, 64], F32)
    zeros64 = sb("zeros64", [64, 128], F32)
    hkf = sb("hkf", [128, 128], F32)
    hkb = sb("hkb", [128, 128], BF16)
    hvf = sb("hvf", [64, 256], F32)
    hvb = sb("hvb", [64, 256], BF16)
    ostg = sb("ostg", [128, 256], F32)
    stt = sb("stt", [128, 8], F32)

    gnws = sb("gnws", [1, 256], F32)
    gnwb = sb("gnwb", [128, 256], F32)
    ones1f = sb("ones1f", [1, 128], F32)
    ident_s = sb("ident_s", [128, 128], BF16)
    identf_s = sb("identf_s", [64, 64], F32)
    triu_s = sb("triu_s", [128, 128], BF16)
    r127_s = sb("r127_s", [128, 128], F32)
    wq_s = sb("wq_s", [128, 8, 2, 256], BF16)
    wk_s = sb("wk_s", [128, 8, 2, 256], BF16)
    wv_s = sb("wv_s", [128, 8, 2, 256], BF16)
    wf_s = sb("wf_s", [128, 8, 2, 64], BF16)
    lnb = {
        "D": D,
        "x": [sb(f"lnx{i}", [128, D], F32) for i in range(2)],
        "xsq": [sb("lnq0", [128, D], F32)] * 2,
        "st": [sb(f"lns{i}", [128, 8], F32) for i in range(2)],
        "xn": [sb(f"lnn{i}", [128, D], BF16) for i in range(2)],
    }

    dbgb = lnb["xsq"][0][:, 512:640]
    hkt = lnb["xsq"][0][:, 768:896]
    hvt = lnb["xsq"][0][0:64, 0:256]
    Ebuf = lnb["xsq"][0][:, 896:960]
    lg = lnb["xsq"][0][:, 960:1024]
    Abf = VTs[:, 0, 0:128]
    Wbf = VTs[:, 0, 128:256]
    Ehb = VTs[:, 0, 256:320]
    ETb = VTs[0:64, 0, 320:448]

    ps = nc.alloc_psum_tensor
    PP = [ps(f"PP{i}", [128, 512], F32) for i in range(2)]
    PAK = ps("PAK", [128, 256], F32)
    PW = ps("PW", [128, 128], F32)
    PCV = ps("PCV", [128, 448], F32)
    PO = ps("PO", [128, 256], F32)
    PT = ps("PT", [128, 2048], BF16)

    ld = p.sem("ld")
    gld = p.sem("gld")
    vch = p.sem("vch")
    sch = p.sem("sch")
    mm = p.sem("mm")
    tr = p.sem("tr")
    ev = p.sem("ev")
    sto = p.sem("sto")
    gmc = p.sem("gmc")

    cv = setup_consts(p, [-1.0, 1.0, 1e-5], gmc)
    p.wait("scalar", gmc, cv)

    p.dma("sync", ident_s[:], ident, ld)
    p.dma("sync", identf_s[:], identf[0:64, 0:64], ld)
    p.dma("sync", triu_s[:], triu, ld)
    p.dma("sync", r127_s[:], r127, ld)
    p.dma("sync", gnws[:], gnw, ld)
    p.dma("gpsimd", wq_s[:], wq.rearrange("(k p) h c -> p k h c", p=128), gld)
    p.dma("gpsimd", wk_s[:], wk.rearrange("(k p) h c -> p k h c", p=128), gld)
    p.dma("gpsimd", wv_s[:], wv.rearrange("(k p) h c -> p k h c", p=128), gld)
    p.dma("gpsimd", wf_s[:], wf.rearrange("(k p) h c -> p k h c", p=128), gld)
    p.emit("gpsimd", lambda g: g.memset(zeros64[:], 0.0), inc=((gmc, 1),))
    p.emit("gpsimd", lambda g: g.memset(ones1f[:], 1.0), inc=((gmc, 1),))
    consts_ld = ld.n
    wld_v = gld.n
    memv = gmc.n
    p.wait("tensor", gld, wld_v)
    p.wait("tensor", gmc, memv)
    p.wait("vector", gld, wld_v)
    p.wait("vector", gmc, memv)

    p.emit("tensor", lambda t: t.matmul(PO[:, 0:256], ones1f[:], gnws[:],
                                        start=True, stop=True),
           wait=[(ld, consts_ld)], inc=((mm, 1),))
    p.emit("vector", lambda v: v.tensor_copy(gnwb[:], PO[:, 0:256]),
           wait=[(mm, mm.n)], inc=((vch, 1),))
    gnw_ready = (vch, vch.n)

    def load_tile(i, xbuf, waits):
        p.dma("sync", xbuf[:], x1b[i * 128:(i + 1) * 128, :], ld, wait=waits)

    def out_tile(i, xn_ap, ready):
        for k in range(8):
            half = k % 2
            w = [ready, (ld, consts_ld)]
            if i > 0 or k >= 2:
                w.append((tr, tr.n - 1))
            p.emit("tensor", lambda t, xn_ap=xn_ap, k=k, half=half:
                   t.transpose(PT[:, half * 1024:half * 1024 + 128],
                               xn_ap[:, k * 128:(k + 1) * 128], ident_s[:]),
                   wait=w, inc=((mm, 1),))
            p.emit("scalar", lambda s, i=i, k=k, half=half:
                   s.copy(xnT[:, k, i * 128:(i + 1) * 128],
                          PT[:, half * 1024:half * 1024 + 128]),
                   wait=[(mm, mm.n)], inc=((tr, 1),))
        return (tr, tr.n)

    emit_ln_tiles(p, NCH, load_tile, out_tile, lnb,
                  dict(ld=ld, vch=vch, sch=sch))
    xnT_ready = (tr, tr.n)

    def chain(eng, fn, extra=(), engsem=None):
        s_ = engsem or (vch if eng == "vector" else sch)
        o_ = sch if s_ is vch else vch
        p.emit(eng, fn, wait=[(vch, vch.n), (sch, sch.n)] + list(extra),
               inc=((s_, 1),))

    for hl in range(2):
        zw = [(vch, vch.n)] if hl > 0 else []
        p.emit("gpsimd", lambda g: g.memset(hkf[:], 0.0), wait=zw,
               inc=((gmc, 1),))
        p.emit("gpsimd", lambda g: g.memset(hkb[:], 0.0), inc=((gmc, 1),))
        p.emit("gpsimd", lambda g: g.memset(hvf[:], 0.0), inc=((gmc, 1),))
        p.emit("gpsimd", lambda g: g.memset(hvb[:], 0.0), inc=((gmc, 1),))
        zs_v = gmc.n

        for sl in range(NSL):
            for kc in range(2):
                for which, wsrc, dst in (("q", wq_s, QT), ("k", wk_s, KT),
                                         ("v", wv_s, None)):
                    pp = PP[0]
                    waits = [xnT_ready, (ld, consts_ld), (ev, ev.n)]
                    for k in range(8):
                        p.emit("tensor", lambda t, k=k, sl=sl, kc=kc, hl=hl,
                               wsrc=wsrc, pp=pp:
                               t.matmul(pp[:], wsrc[:, k, hl,
                                                    kc * 128:(kc + 1) * 128]
                                        if wsrc is not wf_s else None,
                                        xnT[:, k, sl * 512:(sl + 1) * 512],
                                        start=(k == 0), stop=(k == 7)),
                               wait=waits if k == 0 else [],
                               inc=((mm, 1),) if k == 7 else ())
                    mv = mm.n
                    if which == "q":
                        p.emit("scalar", lambda s, pp=pp: s.activation(
                            lnb["xsq"][0][:, 0:512], pp[:], AF.Silu),
                            wait=[(mm, mv)], inc=((sch, 1),))
                        p.emit("vector", lambda v, sl=sl, kc=kc: v.tensor_scalar(
                            QT[:, kc, sl * 512:(sl + 1) * 512], lnb["xsq"][0][:, 0:512],
                            1.0 / 16.0, None, ALU.mult),
                            wait=[(sch, sch.n)], inc=((ev, 1),))
                    elif which == "k":
                        p.emit("scalar", lambda s, sl=sl, kc=kc, pp=pp:
                               s.activation(KT[:, kc, sl * 512:(sl + 1) * 512],
                                            pp[:], AF.Silu),
                               wait=[(mm, mv)], inc=((ev, 1),))
                    else:
                        p.emit("vector", lambda v, kc=kc, pp=pp: v.tensor_copy(
                            VTs[:, kc, :], pp[:]),
                            wait=[(mm, mv)], inc=((ev, 1),))
            for tb in range(4):
                ch_ = sl * 4 + tb
                for kc in range(2):
                    half = (tb * 2 + kc) % 2
                    w = [(ev, ev.n)]
                    if sl > 0 or tb > 0 or kc > 0:
                        w.append((tr, tr.n - 1))
                    p.emit("tensor", lambda t, tb=tb, kc=kc, half=half:
                           t.transpose(PT[:, half * 1024:half * 1024 + 128],
                                       VTs[:, kc, tb * 128:(tb + 1) * 128],
                                       ident_s[:]),
                           wait=w, inc=((mm, 1),))
                    p.emit("scalar", lambda s, ch_=ch_, kc=kc, half=half:
                           s.copy(Vtok[:, ch_, kc * 128:(kc + 1) * 128],
                                  PT[:, half * 1024:half * 1024 + 128]),
                           wait=[(mm, mm.n)], inc=((tr, 1),))
        for ch_ in range(NCH):
            for kc in range(2):
                half = (ch_ * 2 + kc) % 2
                w = [(ev, ev.n), (tr, tr.n - 1)]
                p.emit("tensor", lambda t, ch_=ch_, kc=kc, half=half:
                       t.transpose(PT[:, half * 1024:half * 1024 + 128],
                                   KT[:, kc, ch_ * 128:(ch_ + 1) * 128],
                                   ident_s[:]),
                       wait=w, inc=((mm, 1),))
                p.emit("scalar", lambda s, ch_=ch_, kc=kc, half=half:
                       s.copy(Ktok[:, ch_, kc * 128:(kc + 1) * 128],
                              PT[:, half * 1024:half * 1024 + 128]),
                       wait=[(mm, mm.n)], inc=((tr, 1),))

        for sl in range(NSL):
            pp = PP[1]
            waits = [xnT_ready, (ld, consts_ld), (vch, vch.n), (sch, sch.n)]
            for k in range(8):
                p.emit("tensor", lambda t, k=k, sl=sl, hl=hl, pp=pp:
                       t.matmul(pp[0:64, :], wf_s[:, k, hl, :],
                                xnT[:, k, sl * 512:(sl + 1) * 512],
                                start=(k == 0), stop=(k == 7)),
                       wait=waits if k == 0 else [],
                       inc=((mm, 1),) if k == 7 else ())
            p.emit("vector", lambda v, pp=pp: v.tensor_copy(ga, pp[0:64, :]),
                   wait=[(mm, mm.n)], inc=((vch, 1),))
            chain("scalar", lambda s: s.activation(gb, ga, AF.Abs))
            chain("scalar", lambda s: s.activation(gb, gb, AF.Exp,
                                                   scale=-1.0))
            chain("scalar", lambda s: s.activation(gb, gb, AF.Ln,
                                                   bias=1.0))
            chain("vector", lambda v: v.tensor_scalar(gd, ga, 0.0, None,
                                                      ALU.min))
            chain("vector", lambda v: v.tensor_tensor(gd, gd, gb,
                                                      ALU.subtract))
            chain("vector", lambda v: v.tensor_scalar(ga, gd, 0.125,
                                                      None, ALU.mult))
            chain("scalar", lambda s: s.activation(gd, ga, AF.Exp))
            chain("vector", lambda v: v.tensor_scalar(gd, gd, -1.0, 1.0,
                                                      ALU.mult, ALU.add))
            for cb in range(4):
                chain("vector", lambda v, cb=cb: v.tensor_tensor_scan(
                    gb[:, cb * 128:(cb + 1) * 128],
                    ga[:, cb * 128:(cb + 1) * 128], zeros64[:], 0.0,
                    ALU.add, ALU.add))
            chain("scalar", lambda s: s.activation(gc, gb, AF.Exp))
            chain("scalar", lambda s: s.activation(ga, gb, AF.Exp,
                                                   scale=-1.0))
            chain("vector", lambda v: v.tensor_mul(gb, gd, ga))
            chain("vector", lambda v, sl=sl: v.tensor_copy(
                STbf[:, sl * 512:(sl + 1) * 512], gb))
            for cb in range(4):
                ch_ = sl * 4 + cb
                chain("vector", lambda v, cb=cb, ch_=ch_: v.tensor_copy(
                    tLc[:, ch_:ch_ + 1], gc[:, cb * 128 + 127:cb * 128 + 128]))
            gate_v = vch.n
            gate_s = sch.n
            for cb in range(4):
                ch_ = sl * 4 + cb
                half = cb % 2
                w = [(vch, gate_v), (sch, gate_s)]
                if sl > 0 or cb >= 2:
                    w.append((tr, tr.n - 1))
                p.emit("tensor", lambda t, ch_=ch_, cb=cb, half=half:
                       t.transpose(PT[:, half * 1024:half * 1024 + 64],
                                   STbf[:, ch_ * 128:(ch_ + 1) * 128],
                                   ident_s[0:64, 0:64]),
                       wait=w, inc=((mm, 1),))
                p.emit("scalar", lambda s, ch_=ch_, half=half:
                       s.copy(Stok[:, ch_, :],
                              PT[:, half * 1024:half * 1024 + 64]),
                       wait=[(mm, mm.n)], inc=((tr, 1),))
                p.emit("tensor", lambda t, cb=cb:
                       t.transpose(PW[:, 0:64], gc[:, cb * 128:(cb + 1) * 128],
                                   identf_s[:]),
                       wait=[(tr, tr.n)], inc=((mm, 1),))
                p.emit("vector", lambda v, ch_=ch_: v.tensor_copy(
                    Ttok[:, ch_, :], PW[:, 0:64]),
                    wait=[(mm, mm.n)], inc=((vch, 1),))
                p.wait("tensor", vch, vch.n)

        if debug and hl == 0:
            p.dma("gpsimd", dbg_stbf, STbf[:], gld,
                  wait=[(vch, vch.n), (sch, sch.n), (tr, tr.n)])
            p.dma("gpsimd", dbg_stok, Stok[:], gld,
                  wait=[(vch, vch.n), (sch, sch.n), (tr, tr.n)])
            p.dma("sync", dbg_ttok, Ttok[:], sto,
                  wait=[(vch, vch.n), (sch, sch.n), (tr, tr.n)])
            p.wait("vector", sto, sto.n)
            p.wait("vector", gld, gld.n)
        p.wait("tensor", gmc, zs_v)
        p.wait("vector", gmc, zs_v)
        for ch_ in range(NCH):
            csl = slice(ch_ * 128, (ch_ + 1) * 128)
            for kc in range(2):
                p.emit("tensor", lambda t, kc=kc, csl=csl:
                       t.matmul(PAK[:, 0:128], KT[:, kc, csl], QT[:, kc, csl],
                                start=(kc == 0), stop=(kc == 1)),
                       wait=[(vch, vch.n), (sch, sch.n)] if kc == 0 else [],
                       inc=((mm, 1),) if kc == 1 else ())
            p.emit("vector", lambda v: v.tensor_tensor(Abf, PAK[:, 0:128],
                                                       triu_s[:], ALU.mult),
                   wait=[(mm, mm.n)], inc=((vch, 1),))
            abf_v = vch.n
            if debug and hl == 0 and ch_ < 4:
                p.emit("vector", lambda v: v.tensor_copy(dbgb[:], Abf),
                       wait=[(vch, abf_v)], inc=((vch, 1),))
                p.dma("sync", dbg_ab[ch_], dbgb[:], sto, wait=[(vch, vch.n)])
                p.wait("vector", sto, sto.n)
            p.emit("tensor", lambda t, ch_=ch_: t.matmul(
                PCV[:, 0:64], Abf, Stok[:, ch_, :], start=True, stop=False),
                wait=[(vch, abf_v)])
            for kc in range(2):
                p.emit("tensor", lambda t, kc=kc, csl=csl: t.matmul(
                    PCV[:, 0:64], QT[:, kc, csl], hkb[:, kc * 64:(kc + 1) * 64],
                    start=False, stop=(kc == 1)),
                    inc=((mm, 1),) if kc == 1 else ())
            p.emit("vector", lambda v, ch_=ch_: v.tensor_tensor(
                lg, PCV[:, 0:64], Ttok[:, ch_, :], ALU.mult),
                wait=[(mm, mm.n)], inc=((vch, 1),))
            if debug and hl == 0 and ch_ < 4:
                p.dma("sync", dbg_lg[ch_], lg, sto, wait=[(vch, vch.n)])
                p.wait("vector", sto, sto.n)
            p.emit("vector", lambda v: v.reduce_max(stt[:, 0:1], lg,
                                                    axis=AX.X),
                   wait=[(vch, vch.n)], inc=((vch, 1),))
            p.emit("vector", lambda v: v.tensor_scalar(
                lg, lg, stt[:, 0:1], None, ALU.subtract),
                wait=[(vch, vch.n)], inc=((vch, 1),))
            p.emit("scalar", lambda s: s.activation(
                Ebuf, lg, AF.Exp, accum_out=stt[:, 1:2]),
                wait=[(vch, vch.n)], inc=((sch, 1),))
            p.emit("vector", lambda v: v.reciprocal(stt[:, 2:3], stt[:, 1:2]),
                   wait=[(sch, sch.n)], inc=((vch, 1),))
            p.emit("vector", lambda v, ch_=ch_: v.tensor_tensor(
                Ehb, Ebuf, Ttok[:, ch_, :], ALU.mult),
                wait=[(vch, vch.n - 1), (sch, sch.n)], inc=((vch, 1),))
            ehb_v = vch.n
            if debug and hl == 0 and ch_ < 4:
                p.emit("vector", lambda v: v.tensor_copy(dbgb[:, 0:64], Ehb),
                       wait=[(vch, ehb_v)], inc=((vch, 1),))
                p.dma("sync", dbg_eh[ch_], dbgb[:, 0:64], sto,
                      wait=[(vch, vch.n)])
                p.wait("vector", sto, sto.n)
            p.emit("tensor", lambda t: t.transpose(PT[0:64, 0:128], Ehb,
                                                   ident_s[:]),
                   wait=[(vch, ehb_v), (tr, tr.n)], inc=((mm, 1),))
            p.emit("scalar", lambda s: s.copy(ETb, PT[0:64, 0:128]),
                   wait=[(mm, mm.n)], inc=((tr, 1),))
            etb_t = tr.n
            if debug and hl == 0 and ch_ < 4:
                p.emit("vector", lambda v: v.tensor_copy(dbgb[0:64, :], ETb),
                       wait=[(tr, etb_t)], inc=((vch, 1),))
                p.dma("sync", dbg_etb[ch_], dbgb[0:64, :], sto,
                      wait=[(vch, vch.n)])
                p.wait("vector", sto, sto.n)
                p.wait("scalar", sto, sto.n)
            p.emit("tensor", lambda t, csl=csl: t.matmul(
                PW[:], STbf[:, csl.start:csl.stop], ETb,
                start=True, stop=True),
                wait=[(tr, etb_t)], inc=((mm, 1),))
            if debug and hl == 0 and ch_ < 4:
                p.emit("vector", lambda v: v.tensor_copy(dbgb[:], PW[:]),
                       wait=[(mm, mm.n)], inc=((vch, 1),))
                p.dma("sync", dbg_pw[ch_], dbgb[:], sto, wait=[(vch, vch.n)])
                p.wait("vector", sto, sto.n)
            p.emit("vector", lambda v: v.tensor_tensor(Wbf, PW[:],
                                                       triu_s[:], ALU.mult),
                   wait=[(mm, mm.n)], inc=((vch, 1),))
            wbf_v = vch.n
            if debug and hl == 0 and ch_ < 4:
                p.emit("vector", lambda v: v.tensor_copy(dbgb[:], Wbf),
                       wait=[(vch, wbf_v)], inc=((vch, 1),))
                p.dma("sync", dbg_wb[ch_], dbgb[:], sto, wait=[(vch, vch.n)])
                p.wait("vector", sto, sto.n)
            p.emit("tensor", lambda t, ch_=ch_: t.matmul(
                PO[:], Wbf, Vtok[:, ch_, :], start=True, stop=False),
                wait=[(vch, wbf_v)])
            p.emit("tensor", lambda t: t.matmul(
                PO[:], ETb, hvb[:], start=False, stop=True),
                inc=((mm, 1),))
            o_mm = mm.n
            p.emit("scalar", lambda s: s.activation(
                lnb["xsq"][0][:, 512:768], PO[:], AF.Square, accum_out=stt[:, 3:4]),
                wait=[(mm, o_mm)], inc=((sch, 1),))
            p.emit("vector", lambda v: v.tensor_tensor(
                stt[:, 4:5], stt[:, 2:3], stt[:, 2:3], ALU.mult),
                wait=[(sch, sch.n)], inc=((vch, 1),))
            p.emit("vector", lambda v: v.tensor_tensor(
                stt[:, 4:5], stt[:, 3:4], stt[:, 4:5], ALU.mult),
                wait=[(vch, vch.n)], inc=((vch, 1),))
            p.emit("vector", lambda v: v.tensor_scalar(
                stt[:, 4:5], stt[:, 4:5], 1.0 / 256.0, None, ALU.mult),
                wait=[(vch, vch.n)], inc=((vch, 1),))
            p.emit("scalar", lambda s: s.activation(
                stt[:, 5:6], stt[:, 4:5], AF.Sqrt, bias=1e-5),
                wait=[(vch, vch.n)], inc=((sch, 1),))
            p.emit("vector", lambda v: v.reciprocal(stt[:, 6:7], stt[:, 5:6]),
                   wait=[(sch, sch.n)], inc=((vch, 1),))
            p.emit("vector", lambda v: v.tensor_tensor(
                stt[:, 7:8], stt[:, 6:7], stt[:, 2:3], ALU.mult),
                wait=[(vch, vch.n)], inc=((vch, 1),))
            p.emit("vector", lambda v: v.tensor_scalar(
                ostg[:], PO[:], stt[:, 7:8], None, ALU.mult),
                wait=[(vch, vch.n)], inc=((vch, 1),))
            p.emit("vector", lambda v: v.tensor_tensor(
                ostg[:], ostg[:], gnwb[:], ALU.mult),
                wait=[(vch, vch.n), gnw_ready], inc=((ev, 1),))
            ow = [(ev, ev.n)]
            if ch_ >= 2:
                ow.append((sto, sto.n - 16))
            p.dma("sync", on_out[csl, hl, :], ostg[:], sto, wait=ow)
            for kc in range(2):
                p.emit("tensor", lambda t, kc=kc, ch_=ch_: t.matmul(
                    PAK[:, 128 + kc * 64:128 + (kc + 1) * 64],
                    Ktok[:, ch_, kc * 128:(kc + 1) * 128], Stok[:, ch_, :],
                    start=True, stop=True),
                    wait=[(sch, sch.n)] if kc == 0 else [],
                    inc=((mm, 1),) if kc == 1 else ())
            hk_mm = mm.n
            p.emit("tensor", lambda t, ch_=ch_: t.matmul(
                PCV[0:64, 192:448], Stok[:, ch_, :], Vtok[:, ch_, :],
                start=True, stop=True), inc=((mm, 1),))
            hv_mm = mm.n
            p.emit("tensor", lambda t, ch_=ch_: t.matmul(
                PCV[:, 64:128], r127_s[:], Ttok[:, ch_, :],
                start=True, stop=True), inc=((mm, 1),))
            tl_mm = mm.n
            p.emit("vector", lambda v: v.tensor_tensor(
                hkt, hkf[:], PAK[:, 128:256], ALU.add),
                wait=[(mm, hk_mm)], inc=((vch, 1),))
            p.emit("vector", lambda v: v.tensor_tensor(
                hkf[:, 0:64], hkt[:, 0:64], PCV[:, 64:128], ALU.mult),
                wait=[(vch, vch.n), (mm, tl_mm)], inc=((vch, 1),))
            p.emit("vector", lambda v: v.tensor_tensor(
                hkf[:, 64:128], hkt[:, 64:128], PCV[:, 64:128], ALU.mult),
                wait=[(vch, vch.n)], inc=((vch, 1),))
            p.emit("vector", lambda v: v.tensor_copy(hkb[:], hkf[:]),
                   wait=[(vch, vch.n)], inc=((vch, 1),))
            hkb_v = vch.n
            p.emit("vector", lambda v: v.tensor_tensor(
                hvt, hvf[:], PCV[0:64, 192:448], ALU.add),
                wait=[(mm, hv_mm)], inc=((vch, 1),))
            p.emit("vector", lambda v, ch_=ch_: v.tensor_scalar(
                hvf[:], hvt, tLc[:, ch_:ch_ + 1], None, ALU.mult),
                wait=[(vch, vch.n)], inc=((vch, 1),))
            p.emit("vector", lambda v: v.tensor_copy(hvb[:], hvf[:]),
                   wait=[(vch, vch.n)], inc=((vch, 1),))
            p.wait("tensor", vch, vch.n)
    p.wait("sync", sto, sto.n)
    p.finalize()
    return nc



def build_l1_router(debug=False):
    nc = new_nc()
    p = Prog(nc)
    TOK = TOK_PER_CORE
    NTILE = TOK // 128

    x1 = nc.dram_tensor("x1", [TOK, D], F32, kind="ExternalInput").ap()
    oT1 = nc.dram_tensor("oT1", [D, TOK], F32, kind="ExternalInput").ap()
    wo = nc.dram_tensor("wo", [D, D], F32, kind="ExternalInput").ap()
    rw = nc.dram_tensor("rw", [D, 8], F32, kind="ExternalInput").ap()
    rb = nc.dram_tensor("rb", [1, 8], F32, kind="ExternalInput").ap()
    ident = nc.dram_tensor("ident", [128, 128], BF16, kind="ExternalInput").ap()
    identf = nc.dram_tensor("identf", [128, 128], F32, kind="ExternalInput").ap()
    s1_out = nc.dram_tensor("s1", [TOK, D], F32, kind="ExternalOutput").ap()
    wexp_out = nc.dram_tensor("wexp", [TOK, 8], F32, kind="ExternalOutput").ap()
    xn1T_out = nc.dram_tensor("xn1T", [8, 128, TOK], BF16,
                              kind="ExternalOutput").ap()
    if debug:
        dbg_r = nc.dram_tensor("dbg_r", [TOK, 32], F32,
                               kind="ExternalOutput").ap()

    sb = nc.alloc_sbuf_tensor
    wos = sb("wos", [128, 8, D], BF16)
    rws = sb("rws", [128, 8, 8], F32)
    rbs = sb("rbs", [1, 8], F32)
    ones1f = sb("ones1f", [1, 128], F32)
    ident_s = sb("ident_s", [128, 128], BF16)
    identf_s = sb("identf_s", [128, 128], F32)
    oTs = [sb(f"oTs{i}", [128, 8, 128], BF16) for i in range(2)]
    xnTf = [sb(f"xnTf{i}", [128, 8, 128], F32) for i in range(2)]
    xnTb = [sb(f"xnTb{i}", [128, 8, 128], BF16) for i in range(2)]
    s1s = [sb(f"s1s{i}", [128, D], F32) for i in range(2)]
    rr = [sb(f"rr{i}", [128, 32], F32) for i in range(2)]
    iota8 = sb("iota8", [128, 8], F32)
    lnb = {
        "D": D,
        "x": [sb(f"lnx{i}", [128, D], F32) for i in range(2)],
        "xsq": [sb("lnq0", [128, D], F32)] * 2,
        "st": [sb(f"lns{i}", [128, 8], F32) for i in range(2)],
        "xn": [sb(f"lnn{i}", [128, D], F32) for i in range(2)],
    }

    ps = nc.alloc_psum_tensor
    PS = [ps(f"PS{i}", [128, D], F32) for i in range(2)]
    PTF = ps("PTF", [128, 1024], F32)
    PR = ps("PR", [128, 8], F32)

    ld = p.sem("ld")
    gld = p.sem("gld")
    gmc = p.sem("gmc")
    vch = p.sem("vch")
    sch = p.sem("sch")
    mm = p.sem("mm")
    tr = p.sem("tr")
    ev = p.sem("ev")
    sto = p.sem("sto")

    cv = setup_consts(p, [1e-5], gmc)
    p.wait("scalar", gmc, cv)
    p.dma("gpsimd", wos[:], wo.rearrange("(k p) c -> p k c", p=128), gld)
    p.dma("sync", rws[:], rw.rearrange("(k p) e -> p k e", p=128), ld)
    p.dma("sync", rbs[:], rb, ld)
    p.dma("sync", ident_s[:], ident, ld)
    p.dma("sync", identf_s[:], identf, ld)
    p.emit("gpsimd", lambda g: g.memset(ones1f[:], 1.0), inc=((gmc, 1),))
    p.emit("gpsimd", lambda g: g.iota(
        iota8[:], [[1, 8]], base=10000, channel_multiplier=0,
        allow_small_or_imprecise_dtypes=True), inc=((gmc, 1),))
    consts_ld = ld.n
    memv = gmc.n
    p.wait("tensor", gld, 16)
    p.wait("tensor", gmc, memv)
    p.wait("vector", gmc, memv)

    def load_tile(i, xbuf, waits):
        p.dma("sync", xbuf[:], x1[i * 128:(i + 1) * 128, :], ld, wait=waits)
        ow = [(mm, mm.n)] if i >= 2 else []
        p.dma("gpsimd", oTs[i % 2][:],
              oT1[:, i * 128:(i + 1) * 128].rearrange("(k p) t -> p k t", p=128),
              gld, wait=ow)

    def out_tile(i, xn_ap, ready):
        pb = i % 2
        ldv = ld.n
        gldv = gld.n
        r = rr[pb]
        for k in range(8):
            half = k % 2
            w = [ready, (ld, consts_ld)]
            if i > 0 or k >= 2:
                w.append((tr, tr.n - 2))
            p.emit("tensor", lambda t, xn_ap=xn_ap, k=k, half=half:
                   t.transpose(PTF[:, half * 512:half * 512 + 128],
                               xn_ap[:, k * 128:(k + 1) * 128], identf_s[:]),
                   wait=w, inc=((mm, 1),))
            mv = mm.n
            p.emit("scalar", lambda s, k=k, pb=pb, half=half:
                   s.copy(xnTf[pb][:, k, :], PTF[:, half * 512:half * 512 + 128]),
                   wait=[(mm, mv)], inc=((tr, 1),))
            p.emit("vector", lambda v, k=k, pb=pb:
                   v.tensor_copy(xnTb[pb][:, k, :], xnTf[pb][:, k, :]),
                   wait=[(tr, tr.n)], inc=((tr, 1),))
        xnt_done = tr.n
        p.dma("sync", xn1T_out[:, :, i * 128:(i + 1) * 128]
              .rearrange("k p t -> p k t"), xnTb[pb][:], sto,
              wait=[(tr, xnt_done)])
        for k in range(8):
            p.emit("tensor", lambda t, k=k, pb=pb: t.matmul(
                PR[:], xnTf[pb][:, k, :], rws[:, k, :],
                start=(k == 0), stop=False),
                wait=[(tr, xnt_done), (vch, vch.n)] if k == 0 else [])
        p.emit("tensor", lambda t: t.matmul(
            PR[:], ones1f[:], rbs[:], start=False, stop=True),
            inc=((mm, 1),))
        rmv = mm.n
        p.emit("vector", lambda v, r=r: v.tensor_scalar(
            r[:, 0:8], PR[:], -50.0, 50.0, ALU.max, ALU.min),
            wait=[(mm, rmv)], inc=((vch, 1),))
        p.emit("scalar", lambda s, r=r: s.activation(
            r[:, 8:16], r[:, 0:8], AF.Sigmoid),
            wait=[(vch, vch.n)], inc=((sch, 1),))
        p.emit("vector", lambda v, r=r: v.reduce_max(
            r[:, 16:17], r[:, 8:16], axis=AX.X),
            wait=[(sch, sch.n)], inc=((vch, 1),))
        p.emit("vector", lambda v, r=r: v.tensor_tensor(
            r[:, 24:32], r[:, 8:16],
            r[:, 16:17].broadcast_to([128, 8]), ALU.subtract),
            wait=[(vch, vch.n)], inc=((vch, 1),))
        p.emit("vector", lambda v, r=r: v.tensor_scalar(
            r[:, 0:8], r[:, 24:32], 0.0, None, ALU.is_ge),
            wait=[(vch, vch.n)], inc=((vch, 1),))
        p.emit("vector", lambda v, r=r: v.scalar_tensor_tensor(
            r[:, 24:32], r[:, 0:8], -10000.0, iota8[:], ALU.mult, ALU.add),
            wait=[(vch, vch.n)], inc=((vch, 1),))
        p.emit("vector", lambda v, r=r: v.tensor_reduce(
            r[:, 17:18], r[:, 24:32], axis=AX.X, op=ALU.min),
            wait=[(vch, vch.n)], inc=((vch, 1),))
        p.emit("vector", lambda v, r=r: v.tensor_scalar(
            r[:, 18:19], r[:, 16:17], 1.0 - 1e-8, 1e-8, ALU.min, ALU.max),
            wait=[(vch, vch.n)], inc=((vch, 1),))
        p.emit("vector", lambda v, r=r: v.tensor_scalar(
            r[:, 19:20], r[:, 18:19], 0.5 + 1e-8, None, ALU.add),
            wait=[(vch, vch.n)], inc=((vch, 1),))
        p.emit("vector", lambda v, r=r: v.tensor_scalar(
            r[:, 20:21], r[:, 19:20], 0.5, 2.0, ALU.max, ALU.min),
            wait=[(vch, vch.n)], inc=((vch, 1),))
        p.emit("vector", lambda v, r=r: v.tensor_tensor(
            r[:, 24:32], iota8[:], r[:, 17:18].broadcast_to([128, 8]),
            ALU.subtract),
            wait=[(vch, vch.n)], inc=((vch, 1),))
        p.emit("vector", lambda v, r=r: v.tensor_scalar(
            r[:, 0:8], r[:, 24:32], 10000.0, None, ALU.is_equal),
            wait=[(vch, vch.n)], inc=((vch, 1),))
        p.emit("vector", lambda v, r=r: v.tensor_tensor(
            r[:, 24:32], r[:, 0:8], r[:, 18:19].broadcast_to([128, 8]),
            ALU.mult),
            wait=[(vch, vch.n)], inc=((vch, 1),))
        p.emit("vector", lambda v, r=r: v.tensor_copy(
            r[:, 31:32], r[:, 20:21]),
            wait=[(vch, vch.n)], inc=((vch, 1),))
        wexp_v = vch.n
        p.dma("sync", wexp_out[i * 128:(i + 1) * 128, :], r[:, 24:32], sto,
              wait=[(vch, wexp_v)])
        if debug:
            p.dma("sync", dbg_r[i * 128:(i + 1) * 128, :], r[:], sto,
                  wait=[(vch, wexp_v)])
            p.wait("vector", sto, sto.n)
        pe = PS[pb]
        for k in range(8):
            waits = []
            if k == 0:
                waits = [(ld, ldv), (gld, gldv)]
                if i >= 2:
                    waits.append((ev, ev.n - 1))
            for half in range(2):
                p.emit("tensor", lambda t, k=k, half=half, pb=pb, pe=pe:
                       t.matmul(pe[:, half * 512:(half + 1) * 512],
                                oTs[pb][:, k, :],
                                wos[:, k, half * 512:(half + 1) * 512],
                                start=(k == 0), stop=(k == 7)),
                       wait=waits if (k == 0 and half == 0) else [],
                       inc=((mm, 1),) if (k == 7 and half == 1) else ())
        smv = mm.n
        sw = [(mm, smv)]
        if i >= 2:
            sw.append((sto, 48 * (i - 1)))
        p.emit("scalar", lambda s, pb=pb, pe=pe: s.copy(s1s[pb][:], pe[:]),
               wait=sw, inc=((ev, 1),))
        p.dma("sync", s1_out[i * 128:(i + 1) * 128, :], s1s[pb][:], sto,
              wait=[(ev, ev.n)])
        return (sto, sto.n)

    emit_ln_tiles(p, NTILE, load_tile, out_tile, lnb,
                  dict(ld=ld, vch=vch, sch=sch))
    p.wait("sync", sto, sto.n)
    p.finalize()
    return nc



def build_l1_experts():
    nc = new_nc()
    p = Prog(nc)
    I = 2816
    NI = I // 128
    NSLAB = NT // 512

    xnT_in = nc.dram_tensor("xnT", [8, 128, NT], BF16, kind="ExternalInput").ap()
    w1t = nc.dram_tensor("w1t", [D, I], F32, kind="ExternalInput").ap()
    w3t = nc.dram_tensor("w3t", [D, I], F32, kind="ExternalInput").ap()
    w2t = nc.dram_tensor("w2t", [I, D], F32, kind="ExternalInput").ap()
    wtok = nc.dram_tensor("wtok", [128, NT // 128], F32,
                          kind="ExternalInput").ap()
    z_out = nc.dram_tensor("z", [NT, D], BF16, kind="ExternalOutput").ap()

    sb = nc.alloc_sbuf_tensor
    w1s = sb("w1s", [128, 8, I], BF16)
    w3s = sb("w3s", [128, 8, I], BF16)
    w2s = sb("w2s", [128, NI, D], BF16)
    xns = [sb(f"xns{i}", [128, 8, 512], BF16) for i in range(2)]
    h1s = sb("h1s", [128, 512], F32)
    prodT = sb("prodT", [128, NI, 512], BF16)
    wts_all = sb("wts_all", [128, NT // 128], F32)
    zs = [sb(f"zs{i}", [128, D], BF16) for i in range(2)]

    ps = nc.alloc_psum_tensor
    PH = [ps(f"PH{i}", [128, 512], F32) for i in range(2)]
    PZ = [ps(f"PZ{i}", [128, D], F32) for i in range(2)]

    ld = p.sem("ld")
    gld = p.sem("gld")
    mm = p.sem("mm")
    ev = p.sem("ev")
    sch = p.sem("sch")
    sto = p.sem("sto")

    p.dma("gpsimd", w1s[:], w1t.rearrange("(k p) i -> p k i", p=128), gld)
    p.dma("gpsimd", w3s[:], w3t.rearrange("(k p) i -> p k i", p=128), gld)
    p.dma("gpsimd", w2s[:], w2t.rearrange("(f p) c -> p f c", p=128), gld)
    p.dma("sync", wts_all[:], wtok, ld)
    p.wait("tensor", gld, gld.n)
    p.wait("vector", ld, ld.n)

    zv_vals = []
    for s in range(NSLAB):
        xs = xns[s % 2]
        w = [(mm, mm.n)] if s >= 2 else []
        p.dma("sync", xs[:], xnT_in[:, :, s * 512:(s + 1) * 512]
              .rearrange("k p t -> p k t"), ld, wait=w)
        ldv = ld.n
        for f in range(NI):
            for which in range(2):
                pb = PH[which]
                wsrc = w1s if which == 0 else w3s
                for k in range(8):
                    waits = []
                    if k == 0:
                        waits = [(ld, ldv)]
                        if s > 0 or f > 0:
                            waits.append((sch, sch.n) if which == 0
                                         else (ev, ev.n))
                    p.emit("tensor", lambda t, f=f, k=k, xs=xs, pb=pb, wsrc=wsrc:
                           t.matmul(pb[:], wsrc[:, k, f * 128:(f + 1) * 128],
                                    xs[:, k, :], start=(k == 0), stop=(k == 7)),
                           wait=waits, inc=((mm, 1),) if k == 7 else ())
                if which == 0:
                    p.emit("scalar", lambda sc, pb=pb: sc.activation(
                        h1s[:], pb[:], AF.Silu),
                        wait=[(mm, mm.n)], inc=((sch, 1),))
            p.emit("vector", lambda v, f=f: v.tensor_tensor(
                prodT[:, f, :], h1s[:], PH[1][:], ALU.mult),
                wait=[(mm, mm.n), (sch, sch.n)], inc=((ev, 1),))
        prod_done = ev.n
        for tt in range(4):
            gt = s * 4 + tt
            pz = PZ[tt % 2]
            for f in range(NI):
                waits = []
                if f == 0:
                    waits = [(ev, prod_done)]
                    if gt >= 2:
                        waits.append((ev, zv_vals[gt - 2]))
                for half in range(2):
                    p.emit("tensor", lambda t, f=f, tt=tt, half=half, pz=pz:
                           t.matmul(pz[:, half * 512:(half + 1) * 512],
                                    prodT[:, f, tt * 128:(tt + 1) * 128],
                                    w2s[:, f, half * 512:(half + 1) * 512],
                                    start=(f == 0), stop=(f == NI - 1)),
                           wait=waits if (f == 0 and half == 0) else [],
                           inc=((mm, 1),) if (f == NI - 1 and half == 1) else ())
            mv = mm.n
            zb = zs[gt % 2]
            wv = [(mm, mv)]
            if gt >= 2:
                wv.append((sto, 16 * (gt - 1)))
            p.emit("vector", lambda v, zb=zb, pz=pz, gt=gt:
                   v.tensor_scalar(zb[:], pz[:], wts_all[:, gt:gt + 1],
                                   None, ALU.mult),
                   wait=wv, inc=((ev, 1),))
            zv_vals.append(ev.n)
            p.dma("sync", z_out[gt * 128:(gt + 1) * 128, :], zb[:], sto,
                  wait=[(ev, ev.n)])
    p.wait("sync", sto, sto.n)
    p.finalize()
    return nc



def build_l1_combine():
    nc = new_nc()
    p = Prog(nc)
    TOK = TOK_PER_CORE
    NTILE = TOK // 128

    x1 = nc.dram_tensor("x1", [TOK, D], F32, kind="ExternalInput").ap()
    s1 = nc.dram_tensor("s1", [TOK, D], F32, kind="ExternalInput").ap()
    zsh = nc.dram_tensor("zsh", [TOK, D], BF16, kind="ExternalInput").ap()
    ze = nc.dram_tensor("ze", [7, TOK, D], BF16, kind="ExternalInput").ap()
    wexp = nc.dram_tensor("wexp", [TOK, 8], F32, kind="ExternalInput").ap()
    out_t = nc.dram_tensor("out", [TOK, D], F32, kind="ExternalOutput").ap()

    sb = nc.alloc_sbuf_tensor
    x1s = [sb(f"x1s{i}", [128, D], F32) for i in range(2)]
    s1s = [sb(f"s1s{i}", [128, D], F32) for i in range(2)]
    shs = [sb(f"shs{i}", [128, D], BF16) for i in range(2)]
    zes = [sb(f"zes{i}", [128, 7, D], BF16) for i in range(2)]
    wxs = [sb(f"wxs{i}", [128, 8], F32) for i in range(2)]
    acc = [sb(f"acc{i}", [128, D], F32) for i in range(2)]
    gg = [sb(f"gg{i}", [128, D], F32) for i in range(2)]
    ot = [sb(f"ot{i}", [128, D], F32) for i in range(2)]

    ld = p.sem("ld")
    vch = p.sem("vch")
    sch = p.sem("sch")
    sto = p.sem("sto")

    for i in range(NTILE):
        pb = i % 2
        sl = slice(i * 128, (i + 1) * 128)
        w = [(sto, 16 * (i - 1))] if i >= 2 else []
        p.dma("sync", x1s[pb][:], x1[sl, :], ld, wait=w)
        p.dma("sync", s1s[pb][:], s1[sl, :], ld)
        p.dma("sync", shs[pb][:], zsh[sl, :], ld)
        p.dma("sync", wxs[pb][:], wexp[sl, :], ld)
        p.dma("sync", zes[pb][:], ze[:, sl, :].rearrange("e p c -> p e c"), ld)
        ldv = ld.n
        p.emit("vector", lambda v, pb=pb: v.tensor_tensor(
            acc[pb][:], zes[pb][:, 0, :], zes[pb][:, 1, :], ALU.add),
            wait=[(ld, ldv), (vch, vch.n)], inc=((vch, 1),))
        for e in range(2, 7):
            p.emit("vector", lambda v, pb=pb, e=e: v.tensor_tensor(
                acc[pb][:], acc[pb][:], zes[pb][:, e, :], ALU.add),
                wait=[(vch, vch.n)], inc=((vch, 1),))
        p.emit("vector", lambda v, pb=pb: v.scalar_tensor_tensor(
            ot[pb][:], shs[pb][:], 0.5, acc[pb][:], ALU.mult, ALU.add),
            wait=[(vch, vch.n)], inc=((vch, 1),))
        p.emit("vector", lambda v, pb=pb: v.reciprocal(
            wxs[pb][:, 0:1], wxs[pb][:, 7:8]),
            wait=[(ld, ldv), (vch, vch.n)], inc=((vch, 1),))
        p.emit("vector", lambda v, pb=pb: v.tensor_scalar(
            acc[pb][:], ot[pb][:], wxs[pb][:, 0:1], None, ALU.mult),
            wait=[(vch, vch.n)], inc=((vch, 1),))
        p.emit("scalar", lambda s, pb=pb: s.activation(
            gg[pb][:], acc[pb][:], AF.Silu),
            wait=[(vch, vch.n)], inc=((sch, 1),))
        p.emit("vector", lambda v, pb=pb: v.tensor_tensor(
            ot[pb][:], s1s[pb][:], gg[pb][:], ALU.mult),
            wait=[(sch, sch.n)], inc=((vch, 1),))
        p.emit("vector", lambda v, pb=pb: v.tensor_tensor(
            ot[pb][:], ot[pb][:], x1s[pb][:], ALU.add),
            wait=[(vch, vch.n)], inc=((vch, 1),))
        p.dma("sync", out_t[sl, :], ot[pb][:], sto, wait=[(vch, vch.n)])
    p.wait("sync", sto, sto.n)
    p.finalize()
    return nc



def _consts_bf16():
    import ml_dtypes
    ident = np.eye(128).astype(ml_dtypes.bfloat16)
    tk = np.arange(128)[:, None]
    tq = np.arange(128)[None, :]
    triu = (tk <= tq).astype(ml_dtypes.bfloat16)
    return ident, triu


def _run(nc, in_maps):
    return run_bass_kernel_spmd(nc, in_maps, core_ids=list(range(N_CORES)))


def kernel(L0_attn_x, L0_attn_W_attn, L0_attn_W_proj, L0_ffn_router_w,
           L0_ffn_router_b, L0_ffn_w1, L0_ffn_b1, L0_ffn_w2, L0_ffn_b2,
           L0_ffn_gamma, L0_ffn_beta, L1_attn_Wq, L1_attn_Wk, L1_attn_Wv,
           L1_attn_Wf, L1_attn_g_norm_w, L1_attn_Wo, L1_ffn_router_w,
           L1_ffn_routing_bias, L1_ffn_sw1, L1_ffn_sw2, L1_ffn_sw3,
           L1_ffn_ew1, L1_ffn_ew2, L1_ffn_ew3):
    import ml_dtypes
    del L0_ffn_router_w, L0_ffn_router_b
    asc = np.ascontiguousarray
    x = np.asarray(L0_attn_x, np.float32)
    W_attn = np.asarray(L0_attn_W_attn, np.float32)
    ident, triu = _consts_bf16()
    identf = np.eye(128).astype(np.float32)
    r127 = np.zeros((128, 128), np.float32)
    r127[127, :] = 1.0

    nc = build_l0_attn()
    in_maps = []
    for c in range(N_CORES):
        b = c // 2
        hh = c % 2
        heads = range(hh * 8, hh * 8 + 8)
        wqk_a = np.stack([np.concatenate(
            [W_attn[:, h * 64:(h + 1) * 64],
             W_attn[:, 1024 + h * 64:1024 + (h + 1) * 64]], axis=1)
            for h in heads])
        wv_a = np.stack([W_attn[:, 2048 + h * 64:2048 + (h + 1) * 64]
                         for h in heads])
        in_maps.append({"xb": asc(x[b]), "wqk": asc(wqk_a), "wv": asc(wv_a),
                        "ident": ident, "triu": triu})
    res = _run(nc, in_maps)
    o0 = np.zeros((NT, D), np.float32)
    xnT_full = np.zeros((8, 128, NT), dtype=ml_dtypes.bfloat16)
    for c in range(N_CORES):
        b = c // 2
        hh = c % 2
        oT = np.asarray(res.results[c]["oT"])
        for hl in range(8):
            h = hh * 8 + hl
            rows = oT[hl // 2, (hl % 2) * 64:(hl % 2) * 64 + 64, :]
            o0[b * T:(b + 1) * T, h * 64:(h + 1) * 64] = rows.T
        if hh == 0:
            xnT_full[:, :, b * T:(b + 1) * T] = np.asarray(
                res.results[c]["xnT_o"])

    nc = build_l0_ffn()
    w1 = asc(np.asarray(L0_ffn_w1, np.float32)[0])
    b1 = asc(np.asarray(L0_ffn_b1, np.float32)[0].reshape(32, 128).T)
    w2 = asc(np.asarray(L0_ffn_w2, np.float32)[0])
    b2 = asc(np.asarray(L0_ffn_b2, np.float32)[0].reshape(1, D))
    in_maps = []
    for c in range(N_CORES):
        sl = slice(c * TOK_PER_CORE, (c + 1) * TOK_PER_CORE)
        in_maps.append({"xnT": asc(xnT_full[:, :, sl]),
                        "w1": w1, "b1": b1, "w2": w2, "b2": b2})
    res = _run(nc, in_maps)
    eo = np.concatenate([np.asarray(res.results[c]["eo"])
                         for c in range(N_CORES)], 0)

    nc = build_l0_gate()
    xf = x.reshape(NT, D)
    gamma = np.asarray(L0_ffn_gamma, np.float32).reshape(1, D)
    beta = np.asarray(L0_ffn_beta, np.float32).reshape(1, D)
    wproj = np.asarray(L0_attn_W_proj, np.float32)
    in_maps = []
    for c in range(N_CORES):
        sl = slice(c * TOK_PER_CORE, (c + 1) * TOK_PER_CORE)
        in_maps.append({"xo": asc(xf[sl]), "eo": asc(eo[sl]),
                        "oT": asc(o0[sl].T), "wproj": wproj,
                        "gamma": gamma, "beta": beta})
    res = _run(nc, in_maps)
    x1 = np.concatenate([np.asarray(res.results[c]["x1"])
                         for c in range(N_CORES)], 0)

    nc = build_l1_attn()
    Wq = np.asarray(L1_attn_Wq, np.float32)
    Wk = np.asarray(L1_attn_Wk, np.float32)
    Wv = np.asarray(L1_attn_Wv, np.float32)
    Wf = np.asarray(L1_attn_Wf, np.float32)
    gnw = np.asarray(L1_attn_g_norm_w, np.float32).reshape(1, 256)
    in_maps = []
    for c in range(N_CORES):
        b = c // 2
        hp = c % 2
        hs = [2 * hp, 2 * hp + 1]
        in_maps.append({
            "x1b": asc(x1[b * T:(b + 1) * T]),
            "wq": asc(np.stack([Wq[:, h * 256:(h + 1) * 256] for h in hs], 1)),
            "wk": asc(np.stack([Wk[:, h * 256:(h + 1) * 256] for h in hs], 1)),
            "wv": asc(np.stack([Wv[:, h * 256:(h + 1) * 256] for h in hs], 1)),
            "wf": asc(np.stack([Wf[:, h * 64:(h + 1) * 64] for h in hs], 1)),
            "gnw": gnw, "ident": ident, "identf": identf,
            "triu": triu, "r127": r127,
        })
    res = _run(nc, in_maps)
    o1 = np.zeros((NT, D), np.float32)
    for c in range(N_CORES):
        b = c // 2
        hp = c % 2
        on = np.asarray(res.results[c]["on"])
        o1[b * T:(b + 1) * T, hp * 512:(hp + 1) * 512] = on.reshape(T, 512)

    nc = build_l1_router()
    rw_pad = np.zeros((D, 8), np.float32)
    rw_pad[:, :7] = np.asarray(L1_ffn_router_w, np.float32).T
    rb_pad = np.full((1, 8), -1e9, np.float32)
    rb_pad[0, :7] = np.asarray(L1_ffn_routing_bias, np.float32)
    wo = np.asarray(L1_attn_Wo, np.float32)
    in_maps = []
    for c in range(N_CORES):
        sl = slice(c * TOK_PER_CORE, (c + 1) * TOK_PER_CORE)
        in_maps.append({"x1": asc(x1[sl]), "oT1": asc(o1[sl].T), "wo": wo,
                        "rw": rw_pad, "rb": rb_pad,
                        "ident": ident, "identf": identf})
    res = _run(nc, in_maps)
    s1 = np.concatenate([np.asarray(res.results[c]["s1"])
                         for c in range(N_CORES)], 0)
    wexp = np.concatenate([np.asarray(res.results[c]["wexp"])
                           for c in range(N_CORES)], 0)
    xn1T_full = np.concatenate([np.asarray(res.results[c]["xn1T"])
                                for c in range(N_CORES)], 2)

    nc = build_l1_experts()
    ew1 = np.asarray(L1_ffn_ew1, np.float32)
    ew2 = np.asarray(L1_ffn_ew2, np.float32)
    ew3 = np.asarray(L1_ffn_ew3, np.float32)
    sw1 = np.asarray(L1_ffn_sw1, np.float32)
    sw2 = np.asarray(L1_ffn_sw2, np.float32)
    sw3 = np.asarray(L1_ffn_sw3, np.float32)
    in_maps = []
    for c in range(N_CORES):
        if c < 7:
            w1e, w2e, w3e = ew1[c], ew2[c], ew3[c]
            wt = asc(wexp[:, c].reshape(-1, 128).T)
        else:
            w1e, w2e, w3e = sw1, sw2, sw3
            wt = asc(np.ones((NT // 128, 128), np.float32).T)
        in_maps.append({"xnT": xn1T_full, "w1t": asc(w1e.T),
                        "w3t": asc(w3e.T), "w2t": asc(w2e.T), "wtok": wt})
    res = _run(nc, in_maps)
    z = [np.asarray(res.results[c]["z"]) for c in range(N_CORES)]

    nc = build_l1_combine()
    in_maps = []
    for c in range(N_CORES):
        sl = slice(c * TOK_PER_CORE, (c + 1) * TOK_PER_CORE)
        in_maps.append({
            "x1": asc(x1[sl]), "s1": asc(s1[sl]), "zsh": asc(z[7][sl]),
            "ze": asc(np.stack([z[e][sl] for e in range(7)])),
            "wexp": asc(wexp[sl]),
        })
    res = _run(nc, in_maps)
    out = np.concatenate([np.asarray(res.results[c]["out"])
                          for c in range(N_CORES)], 0)
    return out.reshape(B, T, D)
